# version 02
# baseline (speedup 1.0000x reference)
import contextlib
import numpy as np
import ml_dtypes
import concourse.bass as bass
import concourse.mybir as mybir
from concourse.bass_utils import run_bass_kernel_spmd
from concourse.bass_core_rust import IndirectOffsetOnAxis

F32 = mybir.dt.float32
BF16 = mybir.dt.bfloat16
I32 = mybir.dt.int32
AF = mybir.ActivationFunctionType
ALU = mybir.AluOpType
AX = mybir.AxisListType
BF = ml_dtypes.bfloat16

NCORES = 8
TOK = 2048
NT = TOK // 128
D = 1024
DC = D // 128

ENGINES = ("sync", "tensor", "vector", "scalar", "gpsimd")


class Sched:

    def __init__(self, nc):
        self.nc = nc
        self.ops = []
        self.cnt = {e: 0 for e in ENGINES}
        self.lw = {}
        self.rd = {}

    def op(self, eng, emit, reads=(), writes=(), inc=1, sem_key=None):
        key = sem_key if sem_key is not None else eng
        deps = []
        for r in reads:
            ev = self.lw.get(r)
            if ev:
                deps.append(ev)
        for w in writes:
            ev = self.lw.get(w)
            if ev:
                deps.append(ev)
            deps.extend(self.rd.get(w, ()))
        self.cnt[key] = self.cnt.get(key, 0) + inc
        ev = (key, self.cnt[key])
        for w in writes:
            self.lw[w] = ev
            self.rd[w] = []
        for r in reads:
            self.rd.setdefault(r, []).append(ev)
        self.ops.append((eng, emit, deps, inc, key))

    def dma(self, out, in_, reads=(), writes=(), queue="sync"):
        if writes:
            key = "dma:" + writes[0]
        else:
            assert reads, "store dma needs a read resource for its semaphore"
            key = "dmaS:" + reads[0]
        self.op(queue, lambda e: e.dma_start(out=out, in_=in_), reads, writes,
                inc=16, sem_key=key)

    def finalize(self, ctx):
        nc = self.nc
        keys = sorted({k for k in self.cnt if self.cnt[k] > 0} | set(ENGINES))
        sems = {k: ctx.enter_context(nc.semaphore(f"s_{ki}"))
                for ki, k in enumerate(keys)}
        block = ctx.enter_context(nc.Block())
        waited = {e: {} for e in ENGINES}
        plans = {e: [] for e in ENGINES}
        for (eng, emit, deps, inc, key) in self.ops:
            ws = []
            for (k2, v) in deps:
                if k2 == eng and eng == "tensor":
                    continue
                if v > waited[eng].get(k2, 0):
                    ws.append((k2, v))
                    waited[eng][k2] = v
            plans[eng].append((ws, emit, inc, key))
        finals = dict(self.cnt)

        def emit_engine(eng):
            eng_plans = plans[eng]

            def body(ob):
                for ws, emit, inc, key in eng_plans:
                    for (k2, v) in ws:
                        ob.wait_ge(sems[k2], v)
                    emit(ob).then_inc(sems[key], inc)
                if eng == "gpsimd":
                    for k2 in keys:
                        if finals.get(k2, 0) > 0:
                            ob.wait_ge(sems[k2], finals[k2])
            return body

        for eng in ENGINES:
            if plans[eng] or eng == "gpsimd":
                getattr(block, eng)(emit_engine(eng))



class ConstPool:

    def __init__(self, s, ctx, cap=32):
        self.s = s
        self.ap = ctx.enter_context(s.nc.sbuf_tensor("c_pool", [128, cap], F32)).ap()
        self.cols = {}
        self.cap = cap

    def get(self, val):
        val = float(val)
        if val not in self.cols:
            j = len(self.cols)
            assert j < self.cap
            self.cols[val] = j
            col = self.ap[:, j:j + 1]
            self.s.op("gpsimd", lambda e: e.memset(col, val), writes=["c_pool"])
        j = self.cols[val]
        return self.ap[:, j:j + 1]


def make_consts(s, ctx, need_identity_bf=True, need_identity_f32=False,
                need_ones_row_bf=True, need_ones_col_f32=True):
    nc = s.nc
    c = {}
    c["pool"] = ConstPool(s, ctx)
    if need_identity_bf:
        t = ctx.enter_context(nc.sbuf_tensor("c_ident_bf", [128, 128], BF16))
        c["ident_bf"] = t.ap()
        s.op("gpsimd", lambda e: e.memset(c["ident_bf"], 0.0), writes=["c_ident_bf"])
        s.op("gpsimd", lambda e: e.affine_select(
            out=c["ident_bf"], in_=c["ident_bf"], compare_op=ALU.not_equal,
            fill=1.0, base=0, pattern=[[-1, 128]], channel_multiplier=1),
            reads=["c_ident_bf"], writes=["c_ident_bf"])
    if need_identity_f32:
        t = ctx.enter_context(nc.sbuf_tensor("c_ident_f32", [128, 128], F32))
        c["ident_f32"] = t.ap()
        s.op("gpsimd", lambda e: e.memset(c["ident_f32"], 0.0), writes=["c_ident_f32"])
        s.op("gpsimd", lambda e: e.affine_select(
            out=c["ident_f32"], in_=c["ident_f32"], compare_op=ALU.not_equal,
            fill=1.0, base=0, pattern=[[-1, 128]], channel_multiplier=1),
            reads=["c_ident_f32"], writes=["c_ident_f32"])
    if need_ones_row_bf:
        t = ctx.enter_context(nc.sbuf_tensor("c_ones_row_bf", [1, 128], BF16))
        c["ones_row_bf"] = t.ap()
        s.op("gpsimd", lambda e: e.memset(c["ones_row_bf"], 1.0), writes=["c_ones_row_bf"])
    if need_ones_col_f32:
        t = ctx.enter_context(nc.sbuf_tensor("c_ones_col_f32", [128, 1], F32))
        c["ones_col_f32"] = t.ap()
        s.op("gpsimd", lambda e: e.memset(c["ones_col_f32"], 1.0), writes=["c_ones_col_f32"])
        t = ctx.enter_context(nc.sbuf_tensor("c_ones_row_f32", [1, 128], F32))
        c["ones_row_f32"] = t.ap()
        s.op("gpsimd", lambda e: e.memset(c["ones_row_f32"], 1.0), writes=["c_ones_row_f32"])
    return c


def emit_ln(s, ctx, name, x_aps, x_res, pool, out_bf_aps=None, out_f32_aps=None,
            out_res_bf=None, out_res_f32=None):
    nc = s.nc
    nt = len(x_aps)
    st_ = ctx.enter_context(nc.sbuf_tensor(f"{name}_stats", [128, 4 * nt], F32))
    stats = st_.ap()
    R = f"{name}_stats"
    xm_t = ctx.enter_context(nc.sbuf_tensor(f"{name}_xm", [128, D], F32))
    sq_t = ctx.enter_context(nc.sbuf_tensor(f"{name}_sq", [128, D], F32))
    xm = xm_t.ap()
    sq = sq_t.ap()
    ssum = lambda i: stats[:, i:i + 1]
    smean = lambda i: stats[:, nt + i:nt + i + 1]
    sssq = lambda i: stats[:, 2 * nt + i:2 * nt + i + 1]
    srstd = lambda i: stats[:, 3 * nt + i:3 * nt + i + 1]

    for i in range(nt):
        s.op("vector", lambda e, i=i: e.reduce_sum(ssum(i), x_aps[i], axis=AX.X),
             reads=[x_res[i]], writes=[R])
    s.op("vector", lambda e: e.tensor_scalar_mul(
        stats[:, nt:2 * nt], stats[:, 0:nt], -1.0 / D), reads=[R], writes=[R])
    for i in range(nt):
        s.op("vector", lambda e, i=i: e.tensor_scalar_add(xm, x_aps[i], smean(i)),
             reads=[x_res[i], R], writes=[f"{name}_xm"])
        s.op("scalar", lambda e, i=i: e.activation(
            out=sq, in_=xm, func=AF.Square, accum_out=sssq(i)),
            reads=[f"{name}_xm"], writes=[f"{name}_sq", R])
        eps_ap = pool.get(1e-5)
        s.op("scalar", lambda e, i=i, b=eps_ap: e.activation(
            out=ssum(i), in_=sssq(i), func=AF.Ln, bias=b,
            scale=1.0 / D),
            reads=[R, "c_pool"], writes=[R])
        s.op("scalar", lambda e, i=i: e.activation(
            out=srstd(i), in_=ssum(i), func=AF.Exp, scale=-0.5),
            reads=[R], writes=[R])
        if out_bf_aps is not None:
            s.op("vector", lambda e, i=i: e.tensor_single_scalar(
                out=out_bf_aps[i], in_=xm, scalar=srstd(i), op=ALU.mult),
                reads=[f"{name}_xm", R], writes=[out_res_bf[i]])
        if out_f32_aps is not None:
            s.op("vector", lambda e, i=i: e.tensor_single_scalar(
                out=out_f32_aps[i], in_=xm, scalar=srstd(i), op=ALU.mult),
                reads=[f"{name}_xm", R], writes=[out_res_f32[i]])
    return stats


def emit_transpose_to_featmajor(s, name, src_aps, src_res, dst_aps, dst_res,
                                ps_ap, ps_res, ident, nt=NT, dc=DC):
    for i in range(nt):
        for c in range(dc):
            s.op("tensor", lambda e, i=i, c=c: e.transpose(
                ps_ap, src_aps[i][:, c * 128:(c + 1) * 128], ident),
                reads=[src_res[i], "c_ident_bf"], writes=[ps_res])
            s.op("vector", lambda e, i=i, c=c: e.tensor_copy(
                out=dst_aps[c][:, i * 128:(i + 1) * 128], in_=ps_ap),
                reads=[ps_res], writes=[dst_res[c]])



def mm_split(s, ps_ap, ps_res, lhs_ap, rhs_ap, reads, start, stop, width=512):
    n = rhs_ap.shape[-1]
    for k0 in range(0, n, width):
        k1 = min(k0 + width, n)
        s.op("tensor", lambda e, k0=k0, k1=k1: e.matmul(
            ps_ap[:, k0:k1], lhs_ap, rhs_ap[:, k0:k1], start=start, stop=stop),
            reads=list(reads), writes=[ps_res])


class LNStream:

    def __init__(self, s, ctx, name, pool, nt=NT, scratch=None):
        self.s = s
        self.name = name
        self.pool = pool
        self.nt = nt
        nc = s.nc
        self.stats = ctx.enter_context(nc.sbuf_tensor(f"{name}_st", [128, 4 * nt], F32)).ap()
        self.R = f"{name}_st"
        self.xm = [ctx.enter_context(nc.sbuf_tensor(f"{name}_xm{j}", [128, D], F32)).ap()
                   for j in range(2)]
        self.scratch = scratch
        self.eps = pool.get(1e-5)

    def tile(self, i, x_ap, x_res, outs):
        s, R, nt = self.s, self.R, self.nt
        st = self.stats
        ssum = st[:, i:i + 1]
        smean = st[:, nt + i:nt + i + 1]
        sssq = st[:, 2 * nt + i:2 * nt + i + 1]
        srstd = st[:, 3 * nt + i:3 * nt + i + 1]
        xm = self.xm[i % 2]
        xmR = f"{self.name}_xm{i % 2}"
        if self.scratch is not None:
            sq, sqR = self.scratch
        else:
            sq = self.xm[(i + 1) % 2]
            sqR = f"{self.name}_xm{(i + 1) % 2}"
        s.op("vector", lambda e: e.reduce_sum(ssum, x_ap, axis=AX.X),
             reads=[x_res], writes=[R])
        s.op("vector", lambda e: e.tensor_scalar_mul(smean, ssum, -1.0 / D),
             reads=[R], writes=[R])
        s.op("vector", lambda e: e.tensor_scalar_add(xm, x_ap, smean),
             reads=[x_res, R], writes=[xmR])
        s.op("scalar", lambda e, sq=sq: e.activation(
            out=sq, in_=xm, func=AF.Square, accum_out=sssq),
            reads=[xmR], writes=[sqR, R])
        s.op("scalar", lambda e, b=self.eps: e.activation(
            out=ssum, in_=sssq, func=AF.Ln, bias=b, scale=1.0 / D),
            reads=[R, "c_pool"], writes=[R])
        s.op("scalar", lambda e: e.activation(
            out=srstd, in_=ssum, func=AF.Exp, scale=-0.5),
            reads=[R], writes=[R])
        for (oap, ores) in outs:
            s.op("vector", lambda e, oap=oap: e.tensor_single_scalar(
                out=oap, in_=xm, scalar=srstd, op=ALU.mult),
                reads=[xmR, R], writes=[ores])



def build_launch_A():
    nc = bass.Bass("TRN2", target_bir_lowering=False, debug=False)
    x = nc.dram_tensor("x", [TOK, D], F32, kind="ExternalInput").ap()
    part = nc.dram_tensor("part", [1, D], F32, kind="ExternalOutput").ap()
    s = Sched(nc)
    with contextlib.ExitStack() as ctx:
        c = make_consts(s, ctx, need_identity_bf=False, need_ones_row_bf=False)
        xt, xr, xnf, xnfr = [], [], [], []
        for i in range(NT):
            t = ctx.enter_context(nc.sbuf_tensor(f"x{i}", [128, D], F32))
            xt.append(t.ap()); xr.append(f"x{i}")
            t2 = ctx.enter_context(nc.sbuf_tensor(f"xn{i}", [128, D], F32))
            xnf.append(t2.ap()); xnfr.append(f"xn{i}")
        for i in range(NT):
            s.dma(xt[i], x[i * 128:(i + 1) * 128, :], writes=[xr[i]])
        emit_ln(s, ctx, "ln", xt, xr, c["pool"], out_f32_aps=xnf, out_res_f32=xnfr)
        ps = ctx.enter_context(nc.psum_tensor("ps_row", [1, D], F32)).ap()
        for i in range(NT):
            for h in range(2):
                s.op("tensor", lambda e, i=i, h=h: e.matmul(
                    ps[:, h * 512:(h + 1) * 512], c["ones_col_f32"],
                    xnf[i][:, h * 512:(h + 1) * 512],
                    start=(i == 0), stop=(i == NT - 1)),
                    reads=[xnfr[i], "c_ones_col_f32"], writes=["ps_row"])
        outp = ctx.enter_context(nc.sbuf_tensor("outp", [1, D], F32)).ap()
        s.op("vector", lambda e: e.tensor_copy(out=outp, in_=ps),
             reads=["ps_row"], writes=["outp"])
        s.dma(part, outp, reads=["outp"])
        s.finalize(ctx)
    return nc



def build_launch_B():
    nc = bass.Bass("TRN2", target_bir_lowering=False, debug=False)
    x = nc.dram_tensor("x", [TOK, D], F32, kind="ExternalInput").ap()
    parts2 = nc.dram_tensor("parts2", [2, D], F32, kind="ExternalInput").ap()
    wvt = nc.dram_tensor("wvt", [D, D], F32, kind="ExternalInput").ap()
    wot = nc.dram_tensor("wot", [D, D], F32, kind="ExternalInput").ap()
    tin = {}
    for p in "qkv":
        tin[p] = dict(
            nodewT=nc.dram_tensor(f"{p}_nodewT", [D, 15], F32, kind="ExternalInput").ap(),
            nodeb=nc.dram_tensor(f"{p}_nodeb", [1, 15], F32, kind="ExternalInput").ap(),
            w1pairs=nc.dram_tensor(f"{p}_w1pairs", [8, D, 128], F32, kind="ExternalInput").ap(),
            b1pairs=nc.dram_tensor(f"{p}_b1pairs", [8, 128], F32, kind="ExternalInput").ap(),
            w2pairs=nc.dram_tensor(f"{p}_w2pairs", [8, 128, D], F32, kind="ExternalInput").ap(),
            b2=nc.dram_tensor(f"{p}_b2", [16, D], F32, kind="ExternalInput").ap(),
        )
    x1_out = nc.dram_tensor("x1", [TOK, D], F32, kind="ExternalOutput").ap()
    qT_out = nc.dram_tensor("qT", [D, TOK], BF16, kind="ExternalOutput").ap()
    kT_out = nc.dram_tensor("kT", [D, TOK], BF16, kind="ExternalOutput").ap()
    v_out = nc.dram_tensor("v", [TOK, D], BF16, kind="ExternalOutput").ap()

    s = Sched(nc)
    with contextlib.ExitStack() as ctx:
        sb = lambda n, sh, dt: ctx.enter_context(nc.sbuf_tensor(n, sh, dt)).ap()
        c = make_consts(s, ctx, need_identity_f32=True)
        pool = c["pool"]

        P0 = ctx.enter_context(nc.psum_tensor("P0", [128, 1024], F32)).ap()
        P1 = ctx.enter_context(nc.psum_tensor("P1", [128, 1024], F32)).ap()
        P2 = ctx.enter_context(nc.psum_tensor("P2", [1, 1024], F32)).ap()
        P3 = ctx.enter_context(nc.psum_tensor("P3", [128, 128], F32)).ap()
        PT = ctx.enter_context(nc.psum_tensor("PT", [128, 128], BF16)).ap()

        p2a = sb("p2a", [1, D], F32)
        p2b = sb("p2b", [1, D], F32)
        vs_row = sb("vs_row", [1, D], F32)
        vs_cols = sb("vs_cols", [128, 8], F32)
        vsum_row = sb("vsum_row", [1, D], F32)
        vsum_cols = sb("vsum_cols", [128, 8], F32)
        ovec_row = sb("ovec_row", [1, D], F32)
        bc = sb("bc", [128, D], F32)
        wch = [sb(f"wch{j}", [128, 512], F32) for j in range(2)]

        s.dma(p2a, parts2[0:1, :], writes=["p2a"])
        s.dma(p2b, parts2[1:2, :], writes=["p2b"])
        s.op("vector", lambda e: e.tensor_add(vs_row, p2a, p2b),
             reads=["p2a", "p2b"], writes=["vs_row"])

        def row_to_cols(row_ap, row_res, cols_ap, cols_res):
            for cc in range(8):
                s.op("tensor", lambda e, cc=cc: e.transpose(
                    P3[:, 0:1], row_ap[:, cc * 128:(cc + 1) * 128],
                    c["ident_f32"][0:1, 0:1]),
                    reads=[row_res, "c_ident_f32"], writes=["P3"])
                s.op("vector", lambda e, cc=cc: e.tensor_copy(
                    out=cols_ap[:, cc:cc + 1], in_=P3[:, 0:1]),
                    reads=["P3"], writes=[cols_res])

        def rowvec_mat(cols_ap, cols_res, w_hbm, out_row_ap, out_row_res, tag):
            for cc in range(8):
                for h in range(2):
                    j = (2 * cc + h) % 2
                    s.dma(wch[j], w_hbm[cc * 128:(cc + 1) * 128, h * 512:(h + 1) * 512],
                          writes=[f"wch{j}"])
                    s.op("tensor", lambda e, cc=cc, h=h, j=j: e.matmul(
                        P2[:, h * 512:(h + 1) * 512], cols_ap[:, cc:cc + 1],
                        wch[j],
                        start=(cc == 0), stop=(cc == 7)),
                        reads=[cols_res, f"wch{j}"], writes=["P2"])
            s.op("vector", lambda e: e.tensor_copy(out=out_row_ap, in_=P2),
                 reads=["P2"], writes=[out_row_res])

        row_to_cols(vs_row, "vs_row", vs_cols, "vs_cols")
        rowvec_mat(vs_cols, "vs_cols", wvt, vsum_row, "vsum_row", "wv")
        row_to_cols(vsum_row, "vsum_row", vsum_cols, "vsum_cols")
        rowvec_mat(vsum_cols, "vsum_cols", wot, ovec_row, "ovec_row", "wo")
        for h in range(2):
            s.op("tensor", lambda e, h=h: e.matmul(
                P0[:, h * 512:(h + 1) * 512], c["ones_row_f32"],
                ovec_row[:, h * 512:(h + 1) * 512], start=True, stop=True),
                reads=["ovec_row", "c_ones_row_f32"], writes=["P0"])
        s.op("vector", lambda e: e.tensor_copy(out=bc, in_=P0),
             reads=["P0"], writes=["bc"])

        xin = [sb(f"xin{j}", [128, D], F32) for j in range(2)]
        x1b = [sb(f"x1b{j}", [128, D], F32) for j in range(2)]
        xn1 = [sb(f"xn1_{i}", [128, D], BF16) for i in range(NT)]
        xn1R = [f"xn1_{i}" for i in range(NT)]
        lns = LNStream(s, ctx, "ln1", pool)
        for i in range(NT):
            j = i % 2
            s.dma(xin[j], x[i * 128:(i + 1) * 128, :], writes=[f"xin{j}"])
            s.op("vector", lambda e, j=j: e.tensor_add(x1b[j], xin[j], bc),
                 reads=[f"xin{j}", "bc"], writes=[f"x1b{j}"])
            s.dma(x1_out[i * 128:(i + 1) * 128, :], x1b[j], reads=[f"x1b{j}"])
            lns.tile(i, x1b[j], f"x1b{j}", [(xn1[i], xn1R[i])])

        xn1T = [sb(f"xn1T_{cc}", [128, TOK], BF16) for cc in range(DC)]
        xn1TR = [f"xn1T_{cc}" for cc in range(DC)]
        emit_transpose_to_featmajor(s, "tr1", xn1, xn1R, xn1T, xn1TR,
                                    PT, "PT", c["ident_bf"])

        U = sb("Uall", [16, 16 * 128], BF16)
        s.op("gpsimd", lambda e: e.memset(U, 0.0), writes=["Uall"])
        s.op("gpsimd", lambda e: e.affine_select(
            out=U, in_=U, compare_op=ALU.not_equal, fill=1.0, base=0,
            pattern=[[-1, 16], [0, 128]], channel_multiplier=1),
            reads=["Uall"], writes=["Uall"])
        iota16 = sb("iota16", [16, 1], F32)
        s.op("gpsimd", lambda e: e.iota(iota16, pattern=[[0, 1]], base=0,
                                        channel_multiplier=1,
                                        allow_small_or_imprecise_dtypes=True),
             writes=["iota16"])
        cmp8 = sb("cmp8", [128, 8], F32)
        for p in range(8):
            s.op("gpsimd", lambda e, p=p: e.memset(cmp8[0:64, p:p + 1], float(2 * p)),
                 writes=["cmp8"])
            s.op("gpsimd", lambda e, p=p: e.memset(cmp8[64:128, p:p + 1], float(2 * p + 1)),
                 writes=["cmp8"])

        nw_f = sb("nw_f", [128, 15], F32)
        nw_bf = sb("nw_bf", [128, 15 * 8], BF16)
        nb_row = sb("nb_row", [1, 15], F32)
        nb_bc = sb("nb_bc", [128, 15], F32)
        s_all = sb("s_all", [128, NT * 15], F32)
        TS = sb("TS", [128, NT * 20], F32)
        leafT = sb("leafT", [16, 128], BF16)
        leaf_bf = sb("leaf_bf", [128, NT], BF16)
        LB = sb("LB", [128, TOK], BF16)
        selT = sb("selT", [16, TOK], BF16)
        w1f = sb("w1f", [128, 128], F32)
        w1bf = sb("w1bf", [128, 8 * 8 * 128], BF16)
        b1c = sb("b1c", [128, 8], F32)
        w2f = sb("w2f", [128, D], F32)
        w2bf = sb("w2bf", [128, 8 * D], BF16)
        b2f = sb("b2f", [16, D], F32)
        b2bf = sb("b2bf", [16, D], BF16)
        Mp = sb("Mp", [128, 1024], BF16)
        hgm = sb("hgm", [128, 8 * 1024], BF16)
        relu_s = sb("relu_s", [128, 1024], BF16)
        stg = [sb(f"stg{j}", [128, 1024], BF16) for j in range(2)]

        def col16(j):
            return TS[:, j * NT:(j + 1) * NT]

        def tt(out, a, b, op, extra_reads=()):
            s.op("vector", lambda e, out=out, a=a, b=b, op=op: e.tensor_tensor(
                out=out, in0=a, in1=b, op=op), reads=["s_all", "TS"] + list(extra_reads),
                writes=["TS"])

        def tree(pfx, out_mode, ti):
            dst = {"q": qT_out, "k": kT_out, "v": v_out}[pfx]
            for cc in range(8):
                s.dma(nw_f, ti["nodewT"][cc * 128:(cc + 1) * 128, :], writes=["nw_f"])
                s.op("vector", lambda e, cc=cc: e.tensor_copy(
                    out=nw_bf[:, 15 * cc:15 * (cc + 1)], in_=nw_f),
                    reads=["nw_f"], writes=["nw_bf"])
            s.dma(nb_row, ti["nodeb"], writes=["nb_row"])
            s.op("tensor", lambda e: e.matmul(P3[:, 0:15], c["ones_row_f32"],
                                              nb_row, start=True, stop=True),
                 reads=["nb_row", "c_ones_row_f32"], writes=["P3"])
            s.op("vector", lambda e: e.tensor_copy(out=nb_bc, in_=P3[:, 0:15]),
                 reads=["P3"], writes=["nb_bc"])
            for i in range(NT):
                for cc in range(8):
                    s.op("tensor", lambda e, i=i, cc=cc: e.matmul(
                        P3[:, 0:15], xn1T[cc][:, i * 128:(i + 1) * 128],
                        nw_bf[:, 15 * cc:15 * (cc + 1)],
                        start=(cc == 0), stop=(cc == 7)),
                        reads=[xn1TR[cc], "nw_bf"], writes=["P3"])
                s.op("vector", lambda e, i=i: e.tensor_add(
                    s_all[:, 15 * i:15 * (i + 1)], P3[:, 0:15], nb_bc),
                    reads=["P3", "nb_bc"], writes=["s_all"])
            sa = s_all.rearrange("p (t n) -> p t n", n=15)
            sn = lambda j: sa[:, :, j]
            b0, b1, b2, b3 = col16(0), col16(1), col16(2), col16(3)
            l1, m0, m1, l2 = col16(4), col16(5), col16(6), col16(7)
            p0, p1, p2, p3 = col16(8), col16(9), col16(10), col16(11)
            q0, q1, l3 = col16(12), col16(13), col16(14)
            t1, t2, leaf = col16(15), col16(16), col16(17)

            def ge(out, a):
                s.op("vector", lambda e, out=out, a=a: e.tensor_single_scalar(
                    out=out, in_=a, scalar=0.0, op=ALU.is_ge),
                    reads=["s_all", "TS"], writes=["TS"])

            def mux(out, a, b, g):
                tt(t1, b, a, ALU.subtract)
                tt(t2, t1, g, ALU.mult)
                tt(out, a, t2, ALU.add)

            ge(b0, sn(0))
            mux(l1, sn(1), sn(2), b0)
            ge(b1, l1)
            mux(m0, sn(3), sn(4), b1)
            mux(m1, sn(5), sn(6), b1)
            mux(l2, m0, m1, b0)
            ge(b2, l2)
            mux(p0, sn(7), sn(8), b2)
            mux(p1, sn(9), sn(10), b2)
            mux(p2, sn(11), sn(12), b2)
            mux(p3, sn(13), sn(14), b2)
            mux(q0, p0, p1, b1)
            mux(q1, p2, p3, b1)
            mux(l3, q0, q1, b0)
            ge(b3, l3)
            s.op("vector", lambda e: e.scalar_tensor_tensor(
                out=leaf, in0=b0, scalar=2.0, in1=b1, op0=ALU.mult, op1=ALU.add),
                reads=["TS"], writes=["TS"])
            s.op("vector", lambda e: e.scalar_tensor_tensor(
                out=leaf, in0=leaf, scalar=2.0, in1=b2, op0=ALU.mult, op1=ALU.add),
                reads=["TS"], writes=["TS"])
            s.op("vector", lambda e: e.scalar_tensor_tensor(
                out=leaf, in0=leaf, scalar=2.0, in1=b3, op0=ALU.mult, op1=ALU.add),
                reads=["TS"], writes=["TS"])
            s.op("vector", lambda e: e.tensor_copy(out=leaf_bf, in_=leaf),
                 reads=["TS"], writes=["leaf_bf"])
            s.op("tensor", lambda e: e.transpose(PT[0:16, :], leaf_bf, c["ident_bf"]),
                 reads=["leaf_bf", "c_ident_bf"], writes=["PT"])
            s.op("vector", lambda e: e.tensor_copy(out=leafT, in_=PT[0:16, 0:128]),
                 reads=["PT"], writes=["leafT"])
            for j in range(NT):
                s.op("tensor", lambda e, j=j: e.matmul(
                    P3, U[:, j * 128:(j + 1) * 128], leafT, start=True, stop=True),
                    reads=["Uall", "leafT"], writes=["P3"])
                s.op("vector", lambda e, j=j: e.tensor_copy(
                    out=LB[:, j * 128:(j + 1) * 128], in_=P3),
                    reads=["P3"], writes=["LB"])
            s.op("vector", lambda e: e.tensor_single_scalar(
                out=selT, in_=LB[0:16, :], scalar=iota16, op=ALU.is_equal),
                reads=["LB", "iota16"], writes=["selT"])
            for p in range(8):
                for cc in range(8):
                    s.dma(w1f, ti["w1pairs"][p, cc * 128:(cc + 1) * 128, :],
                          writes=["w1f"])
                    s.op("vector", lambda e, p=p, cc=cc: e.tensor_copy(
                        out=w1bf[:, (p * 8 + cc) * 128:(p * 8 + cc + 1) * 128],
                        in_=w1f), reads=["w1f"], writes=["w1bf"])
                s.dma(b1c[:, p:p + 1],
                      ti["b1pairs"][p:p + 1, :].rearrange("a b -> b a"),
                      writes=["b1c"])
                s.dma(w2f, ti["w2pairs"][p], writes=["w2f"])
                s.op("vector", lambda e, p=p: e.tensor_copy(
                    out=w2bf[:, p * D:(p + 1) * D], in_=w2f),
                    reads=["w2f"], writes=["w2bf"])
            s.dma(b2f, ti["b2"], writes=["b2f"])
            s.op("vector", lambda e: e.tensor_copy(out=b2bf, in_=b2f),
                 reads=["b2f"], writes=["b2bf"])
            for H in range(2):
                hh = slice(H * 1024, (H + 1) * 1024)
                for p in range(8):
                    s.op("vector", lambda e, p=p, H=H: e.tensor_single_scalar(
                        out=Mp[:, 0:1024], in_=LB[:, hh], scalar=cmp8[:, p:p + 1],
                        op=ALU.is_equal), reads=["LB", "cmp8"], writes=["Mp"])
                    for cc in range(8):
                        mm_split(s, P0, "P0",
                                 w1bf[:, (p * 8 + cc) * 128:(p * 8 + cc + 1) * 128],
                                 xn1T[cc][:, hh], ["w1bf", xn1TR[cc]],
                                 start=(cc == 0), stop=(cc == 7))
                    s.op("scalar", lambda e, p=p: e.activation(
                        out=relu_s, in_=P0, func=AF.Relu, bias=b1c[:, p:p + 1]),
                        reads=["P0", "b1c"], writes=["relu_s"])
                    s.op("vector", lambda e, p=p: e.tensor_mul(
                        hgm[:, p * 1024:(p + 1) * 1024], relu_s, Mp[:, 0:1024]),
                        reads=["relu_s", "Mp"], writes=["hgm"])
                if out_mode == "tok":
                    for it in range(8):
                        gi = H * 8 + it
                        for p in range(8):
                            mm_split(s, P1, "P1",
                                     hgm[:, p * 1024 + it * 128:p * 1024 + (it + 1) * 128],
                                     w2bf[:, p * D:(p + 1) * D], ["hgm", "w2bf"],
                                     start=(p == 0), stop=False)
                        mm_split(s, P1, "P1", selT[:, gi * 128:(gi + 1) * 128],
                                 b2bf, ["selT", "b2bf"], start=False, stop=True)
                        j = gi % 2
                        s.op("vector", lambda e, j=j: e.tensor_copy(
                            out=stg[j], in_=P1), reads=["P1"], writes=[f"stg{j}"])
                        s.dma(dst[gi * 128:(gi + 1) * 128, :], stg[j],
                              reads=[f"stg{j}"])
                else:
                    for do in range(8):
                        for p in range(8):
                            mm_split(s, P1, "P1",
                                     w2bf[:, p * D + do * 128:p * D + (do + 1) * 128],
                                     hgm[:, p * 1024:(p + 1) * 1024], ["hgm", "w2bf"],
                                     start=(p == 0), stop=False)
                        mm_split(s, P1, "P1", b2bf[:, do * 128:(do + 1) * 128],
                                 selT[:, hh], ["selT", "b2bf"], start=False, stop=True)
                        j = do % 2
                        s.op("vector", lambda e, j=j: e.tensor_copy(
                            out=stg[j], in_=P1), reads=["P1"], writes=[f"stg{j}"])
                        s.dma(dst[do * 128:(do + 1) * 128, H * 1024:(H + 1) * 1024],
                              stg[j], reads=[f"stg{j}"])

        tree("q", "feat", tin["q"])
        tree("k", "feat", tin["k"])
        tree("v", "tok", tin["v"])
        s.finalize(ctx)
    return nc


def run_B(inputs, parts):
    nc = _get_prog("B", build_launch_B)
    shards = _shard_tokens(inputs["L0_attn_x"])
    wvt = np.ascontiguousarray(inputs["L0_attn_w_qkv"][2048:3072].T)
    wot = np.ascontiguousarray(inputs["L0_attn_w_o"].T)
    tw = {}
    for p in "qkv":
        w1 = inputs[f"L0_ffn_{p}_w1"]
        w2 = inputs[f"L0_ffn_{p}_w2"]
        b1 = inputs[f"L0_ffn_{p}_b1"]
        tw[f"{p}_nodewT"] = np.ascontiguousarray(inputs[f"L0_ffn_{p}_node_w"].T)
        tw[f"{p}_nodeb"] = np.ascontiguousarray(inputs[f"L0_ffn_{p}_node_b"].reshape(1, 15))
        tw[f"{p}_w1pairs"] = np.ascontiguousarray(
            np.stack([np.concatenate([w1[2 * j], w1[2 * j + 1]], axis=1) for j in range(8)]))
        tw[f"{p}_b1pairs"] = np.ascontiguousarray(
            np.stack([np.concatenate([b1[2 * j], b1[2 * j + 1]]) for j in range(8)]))
        tw[f"{p}_w2pairs"] = np.ascontiguousarray(
            np.stack([np.concatenate([w2[2 * j], w2[2 * j + 1]], axis=0) for j in range(8)]))
        tw[f"{p}_b2"] = np.ascontiguousarray(inputs[f"L0_ffn_{p}_b2"])
    in_maps = []
    for ci in range(NCORES):
        m = {"x": shards[ci], "parts2": parts[2 * (ci // 2):2 * (ci // 2) + 2],
             "wvt": wvt, "wot": wot}
        m.update(tw)
        in_maps.append(m)
    res = run_bass_kernel_spmd(nc, in_maps, core_ids=list(range(NCORES)))
    return {
        "x1": [res.results[i]["x1"] for i in range(NCORES)],
        "qT": [res.results[i]["qT"] for i in range(NCORES)],
        "kT": [res.results[i]["kT"] for i in range(NCORES)],
        "v": [res.results[i]["v"] for i in range(NCORES)],
    }



def build_launch_C():
    nc = bass.Bass("TRN2", target_bir_lowering=False, debug=False)
    qT = nc.dram_tensor("qT", [D, TOK], BF16, kind="ExternalInput").ap()
    kT = nc.dram_tensor("kT", [D, 2 * TOK], BF16, kind="ExternalInput").ap()
    v = nc.dram_tensor("v", [2 * TOK, D], BF16, kind="ExternalInput").ap()
    x1 = nc.dram_tensor("x1", [TOK, D], F32, kind="ExternalInput").ap()
    x2 = nc.dram_tensor("x2", [TOK, D], F32, kind="ExternalOutput").ap()

    s = Sched(nc)
    with contextlib.ExitStack() as ctx:
        sb = lambda n, sh, dt: ctx.enter_context(nc.sbuf_tensor(n, sh, dt)).ap()
        c = make_consts(s, ctx, need_identity_f32=True)
        ones_col_bf = sb("ones_col_bf", [128, 1], BF16)
        s.op("gpsimd", lambda e: e.memset(ones_col_bf, 1.0), writes=["ones_col_bf"])

        PS = ctx.enter_context(nc.psum_tensor("PS", [128, 512], F32)).ap()
        PR = ctx.enter_context(nc.psum_tensor("PR", [1, 512], F32)).ap()
        PO = ctx.enter_context(nc.psum_tensor("PO", [128, 1024], F32)).ap()
        PTR = ctx.enter_context(nc.psum_tensor("PTR", [128, 128], F32)).ap()

        q_sb = [sb(f"q{cc}", [128, TOK], BF16) for cc in range(DC)]
        k_sb = [sb(f"k{cc}", [128, 2 * TOK], BF16) for cc in range(DC)]
        ptb = sb("ptb", [128, 32 * 512], BF16)
        vt = [sb(f"vt{j}", [128, D], BF16) for j in range(2)]
        rs_row = sb("rs_row", [1, 512], F32)
        rr_row = sb("rr_row", [1, 512], F32)
        rcols = sb("rcols", [128, NT], F32)
        x1t = [sb(f"x1t{j}", [128, D], F32) for j in range(2)]
        stg = [sb(f"stg{j}", [128, D], F32) for j in range(2)]

        for cc in range(DC):
            s.dma(q_sb[cc], qT[cc * 128:(cc + 1) * 128, :], writes=[f"q{cc}"])
            s.dma(k_sb[cc], kT[cc * 128:(cc + 1) * 128, :], writes=[f"k{cc}"])

        for qg in range(4):
            qs = slice(qg * 512, (qg + 1) * 512)
            for kt in range(32):
                for cc in range(DC):
                    s.op("tensor", lambda e, kt=kt, cc=cc, qs=qs: e.matmul(
                        PS, k_sb[cc][:, kt * 128:(kt + 1) * 128], q_sb[cc][:, qs],
                        start=(cc == 0), stop=(cc == DC - 1)),
                        reads=[f"k{cc}", f"q{cc}"], writes=["PS"])
                s.op("scalar", lambda e, kt=kt: e.activation(
                    out=ptb[:, kt * 512:(kt + 1) * 512], in_=PS, func=AF.Exp,
                    scale=1.0 / 32.0), reads=["PS"], writes=["ptb"])
            for kt in range(32):
                s.op("tensor", lambda e, kt=kt: e.matmul(
                    PR, ones_col_bf, ptb[:, kt * 512:(kt + 1) * 512],
                    start=(kt == 0), stop=(kt == 31)),
                    reads=["ptb", "ones_col_bf"], writes=["PR"])
            s.op("vector", lambda e: e.tensor_copy(out=rs_row, in_=PR),
                 reads=["PR"], writes=["rs_row"])
            s.op("vector", lambda e: e.reciprocal(out=rr_row, in_=rs_row),
                 reads=["rs_row"], writes=["rr_row"])
            for qt in range(4):
                gi = qg * 4 + qt
                s.op("tensor", lambda e, qt=qt: e.transpose(
                    PTR[:, 0:1], rr_row[:, qt * 128:(qt + 1) * 128],
                    c["ident_f32"][0:1, 0:1]),
                    reads=["rr_row", "c_ident_f32"], writes=["PTR"])
                s.op("vector", lambda e, gi=gi: e.tensor_copy(
                    out=rcols[:, gi:gi + 1], in_=PTR[:, 0:1]),
                    reads=["PTR"], writes=["rcols"])
            for qt in range(4):
                gi = qg * 4 + qt
                j = gi % 2
                for kt in range(32):
                    s.dma(vt[kt % 2], v[kt * 128:(kt + 1) * 128, :],
                          writes=[f"vt{kt % 2}"])
                    for h in range(2):
                        s.op("tensor", lambda e, kt=kt, qt=qt, h=h: e.matmul(
                            PO[:, h * 512:(h + 1) * 512],
                            ptb[:, kt * 512 + qt * 128:kt * 512 + (qt + 1) * 128],
                            vt[kt % 2][:, h * 512:(h + 1) * 512],
                            start=(kt == 0), stop=(kt == 31)),
                            reads=["ptb", f"vt{kt % 2}"], writes=["PO"])
                s.dma(x1t[j], x1[gi * 128:(gi + 1) * 128, :], writes=[f"x1t{j}"])
                s.op("vector", lambda e, gi=gi, j=j: e.tensor_single_scalar(
                    out=stg[j], in_=PO, scalar=rcols[:, gi:gi + 1], op=ALU.mult),
                    reads=["PO", "rcols"], writes=[f"stg{j}"])
                s.op("vector", lambda e, j=j: e.tensor_add(
                    out=stg[j], in0=stg[j], in1=x1t[j]),
                    reads=[f"stg{j}", f"x1t{j}"], writes=[f"stg{j}"])
                s.dma(x2[gi * 128:(gi + 1) * 128, :], stg[j], reads=[f"stg{j}"])
        s.finalize(ctx)
    return nc


def run_C(x1_full, qT_full, kT_full, v_full):
    nc = _get_prog("C", build_launch_C)
    in_maps = []
    for ci in range(NCORES):
        b = ci // 2
        m = {
            "x1": np.ascontiguousarray(x1_full[ci * TOK:(ci + 1) * TOK]),
            "qT": np.ascontiguousarray(qT_full[:, ci * TOK:(ci + 1) * TOK]),
            "kT": np.ascontiguousarray(kT_full[:, b * 4096:(b + 1) * 4096]),
            "v": np.ascontiguousarray(v_full[b * 4096:(b + 1) * 4096]),
        }
        in_maps.append(m)
    res = run_bass_kernel_spmd(nc, in_maps, core_ids=list(range(NCORES)))
    return np.concatenate([res.results[i]["x2"] for i in range(NCORES)], axis=0)


def load_weight_bf(s, ctx, name, hbm_ap, f_out, scratch, scratch_res, nchunk=DC):
    nc = s.nc
    aps, res = [], []
    for cc in range(nchunk):
        t = ctx.enter_context(nc.sbuf_tensor(f"{name}_{cc}", [128, f_out], BF16)).ap()
        aps.append(t)
        res.append(f"{name}_{cc}")
        j = cc % 2
        s.dma(scratch[j], hbm_ap[cc * 128:(cc + 1) * 128, :], writes=[scratch_res[j]])
        s.op("vector", lambda e, cc=cc, j=j: e.tensor_copy(out=aps[cc], in_=scratch[j]),
             reads=[scratch_res[j]], writes=[res[cc]])
    return aps, res



def build_launch_D1():
    nc = bass.Bass("TRN2", target_bir_lowering=False, debug=False)
    x2 = nc.dram_tensor("x2", [TOK, D], F32, kind="ExternalInput").ap()
    wq = nc.dram_tensor("wq", [D, D], F32, kind="ExternalInput").ap()
    wk = nc.dram_tensor("wk", [D, D], F32, kind="ExternalInput").ap()
    wv = nc.dram_tensor("wv", [D, D], F32, kind="ExternalInput").ap()
    qT_o = nc.dram_tensor("qT", [D, TOK], BF16, kind="ExternalOutput").ap()
    kT_o = nc.dram_tensor("kT", [D, TOK], BF16, kind="ExternalOutput").ap()
    v_o = nc.dram_tensor("v", [TOK, D], BF16, kind="ExternalOutput").ap()

    s = Sched(nc)
    with contextlib.ExitStack() as ctx:
        sb = lambda n, sh, dt: ctx.enter_context(nc.sbuf_tensor(n, sh, dt)).ap()
        c = make_consts(s, ctx)
        pool = c["pool"]
        P0 = ctx.enter_context(nc.psum_tensor("P0", [128, 1024], F32)).ap()
        PT = ctx.enter_context(nc.psum_tensor("PT", [128, 128], BF16)).ap()
        xin = [sb(f"xin{j}", [128, D], F32) for j in range(2)]
        xn2 = [sb(f"xn2_{i}", [128, D], BF16) for i in range(NT)]
        xn2R = [f"xn2_{i}" for i in range(NT)]
        xn2T = [sb(f"xn2T_{cc}", [128, TOK], BF16) for cc in range(DC)]
        xn2TR = [f"xn2T_{cc}" for cc in range(DC)]
        wsc = [sb(f"wsc{j}", [128, D], F32) for j in range(2)]
        stg = [sb(f"stg{j}", [128, 1024], BF16) for j in range(2)]

        lns = LNStream(s, ctx, "ln2", pool)
        for i in range(NT):
            j = i % 2
            s.dma(xin[j], x2[i * 128:(i + 1) * 128, :], writes=[f"xin{j}"])
            lns.tile(i, xin[j], f"xin{j}", [(xn2[i], xn2R[i])])
        emit_transpose_to_featmajor(s, "tr2", xn2, xn2R, xn2T, xn2TR,
                                    PT, "PT", c["ident_bf"])

        def proj_feat(w_hbm, wname, dst):
            wb, wbR = load_weight_bf(s, ctx, wname, w_hbm, D, wsc,
                                     ["wsc0", "wsc1"])
            for fb in range(8):
                for tg in range(4):
                    for cc in range(DC):
                        s.op("tensor", lambda e, fb=fb, tg=tg, cc=cc: e.matmul(
                            P0[:, 0:512], wb[cc][:, fb * 128:(fb + 1) * 128],
                            xn2T[cc][:, tg * 512:(tg + 1) * 512],
                            start=(cc == 0), stop=(cc == DC - 1)),
                            reads=[wbR[cc], xn2TR[cc]], writes=["P0"])
                    j = (fb * 4 + tg) % 2
                    s.op("vector", lambda e, j=j: e.tensor_copy(
                        out=stg[j][:, 0:512], in_=P0[:, 0:512]),
                        reads=["P0"], writes=[f"stg{j}"])
                    s.dma(dst[fb * 128:(fb + 1) * 128, tg * 512:(tg + 1) * 512],
                          stg[j][:, 0:512], reads=[f"stg{j}"])

        def proj_tok(w_hbm, wname, dst):
            wb, wbR = load_weight_bf(s, ctx, wname, w_hbm, D, wsc,
                                     ["wsc0", "wsc1"])
            for i in range(NT):
                for h in range(2):
                    for cc in range(DC):
                        s.op("tensor", lambda e, i=i, h=h, cc=cc: e.matmul(
                            P0[:, h * 512:(h + 1) * 512],
                            xn2T[cc][:, i * 128:(i + 1) * 128],
                            wb[cc][:, h * 512:(h + 1) * 512],
                            start=(cc == 0), stop=(cc == DC - 1)),
                            reads=[wbR[cc], xn2TR[cc]], writes=["P0"])
                j = i % 2
                s.op("vector", lambda e, j=j: e.tensor_copy(out=stg[j], in_=P0),
                     reads=["P0"], writes=[f"stg{j}"])
                s.dma(dst[i * 128:(i + 1) * 128, :], stg[j], reads=[f"stg{j}"])

        proj_feat(wq, "wqb", qT_o)
        proj_feat(wk, "wkb", kT_o)
        proj_tok(wv, "wvb", v_o)
        s.finalize(ctx)
    return nc


def run_D1(x2_full, inputs):
    nc = _get_prog("D1", build_launch_D1)
    wq = np.ascontiguousarray(inputs["L1_attn_Wq"].T)
    wk = np.ascontiguousarray(inputs["L1_attn_Wk"].T)
    wv = np.ascontiguousarray(inputs["L1_attn_Wv"].T)
    in_maps = [{"x2": np.ascontiguousarray(x2_full[i * TOK:(i + 1) * TOK]),
                "wq": wq, "wk": wk, "wv": wv} for i in range(NCORES)]
    res = run_bass_kernel_spmd(nc, in_maps, core_ids=list(range(NCORES)))
    return {
        "qT": np.concatenate([res.results[i]["qT"] for i in range(NCORES)], 1),
        "kT": np.concatenate([res.results[i]["kT"] for i in range(NCORES)], 1),
        "v": np.concatenate([res.results[i]["v"] for i in range(NCORES)], 0),
    }



def build_launch_D2():
    nc = bass.Bass("TRN2", target_bir_lowering=False, debug=False)
    qT = nc.dram_tensor("qT", [D, TOK], BF16, kind="ExternalInput").ap()
    kT = nc.dram_tensor("kT", [D, 2 * TOK], BF16, kind="ExternalInput").ap()
    va = nc.dram_tensor("va", [2 * TOK, 16 * 65], BF16, kind="ExternalInput").ap()
    wout = nc.dram_tensor("wout", [D, D], F32, kind="ExternalInput").ap()
    bout = nc.dram_tensor("bout", [1, D], F32, kind="ExternalInput").ap()
    x2 = nc.dram_tensor("x2", [TOK, D], F32, kind="ExternalInput").ap()
    x3 = nc.dram_tensor("x3", [TOK, D], F32, kind="ExternalOutput").ap()

    SC2 = 1.0 / 8.0

    s = Sched(nc)
    with contextlib.ExitStack() as ctx:
        sb = lambda n, sh, dt: ctx.enter_context(nc.sbuf_tensor(n, sh, dt)).ap()
        c = make_consts(s, ctx, need_identity_f32=True)
        PS = ctx.enter_context(nc.psum_tensor("PS", [128, 512], F32)).ap()
        POA = ctx.enter_context(nc.psum_tensor("POA", [65, 512], F32)).ap()
        PB = ctx.enter_context(nc.psum_tensor("PB", [64, 512], F32)).ap()
        P0 = ctx.enter_context(nc.psum_tensor("P0", [128, 1024], F32)).ap()

        q_sb = [sb(f"q{cc}", [128, TOK], BF16) for cc in range(DC)]
        va_sb = [sb(f"va{j}", [128, 16 * 65], BF16) for j in range(2)]
        kth = [sb(f"kth{j}", [128, 2 * TOK], BF16) for j in range(2)]
        ptb = sb("ptb", [128, 32 * 512], BF16)
        aT = [sb(f"aT{cc}", [128, TOK], BF16) for cc in range(DC)]
        r_row = sb("r_row", [1, 512], F32)
        rr_row = sb("rr_row", [1, 512], F32)
        rbc = sb("rbc", [64, 512], F32)
        wsc0 = sb("wsc0", [128, D], F32)
        wsc = [wsc0, wsc0]
        bout_r = sb("bout_r", [1, D], F32)
        bout_bc = sb("bout_bc", [128, D], F32)
        x2t0 = sb("x2t0", [128, D], F32)
        x2t = [x2t0, x2t0]
        stg0 = sb("stg0", [128, D], F32)
        stg = [stg0, stg0]

        for cc in range(DC):
            s.dma(q_sb[cc], qT[cc * 128:(cc + 1) * 128, :], writes=[f"q{cc}"])

        for qg in range(4):
            qs = slice(qg * 512, (qg + 1) * 512)
            for h in range(16):
                j = h % 2
                hc = h // 2
                hr = 64 * (h % 2)
                s.dma(kth[j][hr:hr + 64, :], kT[h * 64:(h + 1) * 64, :],
                      writes=[f"kth{j}"])
                for kt in range(32):
                    s.op("tensor", lambda e, h=h, kt=kt, qs=qs, j=j, hc=hc, hr=hr: e.matmul(
                        PS, kth[j][hr:hr + 64, kt * 128:(kt + 1) * 128],
                        q_sb[hc][hr:hr + 64, qs], start=True, stop=True),
                        reads=[f"kth{j}", f"q{hc}"], writes=["PS"])
                    s.op("scalar", lambda e, kt=kt: e.activation(
                        out=ptb[:, kt * 512:(kt + 1) * 512], in_=PS, func=AF.Exp,
                        scale=SC2), reads=["PS"], writes=["ptb"])
                for kt in range(32):
                    jv = kt % 2
                    s.dma(va_sb[jv], va[kt * 128:(kt + 1) * 128, :],
                          writes=[f"va{jv}"])
                    s.op("tensor", lambda e, h=h, kt=kt, jv=jv: e.matmul(
                        POA, va_sb[jv][:, h * 65:(h + 1) * 65],
                        ptb[:, kt * 512:(kt + 1) * 512],
                        start=(kt == 0), stop=(kt == 31)),
                        reads=[f"va{jv}", "ptb"], writes=["POA"])
                s.op("vector", lambda e: e.tensor_copy(out=r_row, in_=POA[64:65, :]),
                     reads=["POA"], writes=["r_row"])
                s.op("vector", lambda e: e.reciprocal(out=rr_row, in_=r_row),
                     reads=["r_row"], writes=["rr_row"])
                s.op("tensor", lambda e: e.matmul(
                    PB, c["ones_row_f32"][:, 0:64], rr_row, start=True, stop=True),
                    reads=["rr_row", "c_ones_row_f32"], writes=["PB"])
                s.op("vector", lambda e: e.tensor_copy(out=rbc, in_=PB),
                     reads=["PB"], writes=["rbc"])
                s.op("vector", lambda e, h=h, qg=qg, hc=hc, hr=hr: e.tensor_mul(
                    aT[hc][hr:hr + 64, qg * 512:(qg + 1) * 512], POA[0:64, :], rbc),
                    reads=["POA", "rbc"], writes=[f"aT{hc}"])
        wb, wbR = load_weight_bf(s, ctx, "woutb", wout, D, wsc, ["wsc0", "wsc0"])
        s.dma(bout_r, bout, writes=["bout_r"])
        for hh in range(2):
            s.op("tensor", lambda e, hh=hh: e.matmul(
                P0[:, hh * 512:(hh + 1) * 512], c["ones_row_f32"],
                bout_r[:, hh * 512:(hh + 1) * 512], start=True, stop=True),
                reads=["bout_r", "c_ones_row_f32"], writes=["P0"])
        s.op("vector", lambda e: e.tensor_copy(out=bout_bc, in_=P0),
             reads=["P0"], writes=["bout_bc"])
        for i in range(NT):
            for hh in range(2):
                for cc in range(DC):
                    s.op("tensor", lambda e, i=i, hh=hh, cc=cc: e.matmul(
                        P0[:, hh * 512:(hh + 1) * 512],
                        aT[cc][:, i * 128:(i + 1) * 128],
                        wb[cc][:, hh * 512:(hh + 1) * 512],
                        start=(cc == 0), stop=(cc == DC - 1)),
                        reads=[f"aT{cc}", wbR[cc]], writes=["P0"])
            j = i % 2
            s.dma(x2t[j], x2[i * 128:(i + 1) * 128, :], writes=["x2t0"])
            s.op("vector", lambda e, j=j: e.tensor_add(stg[j], P0, bout_bc),
                 reads=["P0", "bout_bc"], writes=["stg0"])
            s.op("vector", lambda e, j=j: e.tensor_add(stg[j], stg[j], x2t[j]),
                 reads=[f"stg{j}", "x2t0"], writes=["stg0"])
            s.dma(x3[i * 128:(i + 1) * 128, :], stg[j], reads=["stg0"])
        s.finalize(ctx)
    return nc


def run_D2(x2_full, qT_full, kT_full, v_full, inputs):
    nc = _get_prog("D2", build_launch_D2)
    ones = np.ones((v_full.shape[0], 1), dtype=v_full.dtype)
    va = np.concatenate(
        [np.concatenate([v_full[:, 64 * h:64 * (h + 1)], ones], axis=1)
         for h in range(16)], axis=1)
    wout = np.ascontiguousarray(inputs["L1_attn_Wout"].T)
    bout = np.ascontiguousarray(inputs["L1_attn_bout"].reshape(1, D))
    in_maps = []
    for ci in range(NCORES):
        b = ci // 2
        in_maps.append({
            "qT": np.ascontiguousarray(qT_full[:, ci * TOK:(ci + 1) * TOK]),
            "kT": np.ascontiguousarray(kT_full[:, b * 4096:(b + 1) * 4096]),
            "va": np.ascontiguousarray(va[b * 4096:(b + 1) * 4096]),
            "wout": wout, "bout": bout,
            "x2": np.ascontiguousarray(x2_full[ci * TOK:(ci + 1) * TOK]),
        })
    res = run_bass_kernel_spmd(nc, in_maps, core_ids=list(range(NCORES)))
    return np.concatenate([res.results[i]["x3"] for i in range(NCORES)], axis=0)



CAP = 2560
NEXP = 16


def build_launch_E1():
    nc = bass.Bass("TRN2", target_bir_lowering=False, debug=False)
    x3 = nc.dram_tensor("x3", [TOK, D], F32, kind="ExternalInput").ap()
    rw = nc.dram_tensor("rw", [D, NEXP], F32, kind="ExternalInput").ap()
    rb = nc.dram_tensor("rb", [1, NEXP], F32, kind="ExternalInput").ap()
    xn3_o = nc.dram_tensor("xn3", [TOK, D], BF16, kind="ExternalOutput").ap()
    e12_o = nc.dram_tensor("e12", [TOK, 2], F32, kind="ExternalOutput").ap()
    w12_o = nc.dram_tensor("w12", [TOK, 2], F32, kind="ExternalOutput").ap()
    cnt_o = nc.dram_tensor("cnt", [1, NEXP], F32, kind="ExternalOutput").ap()

    s = Sched(nc)
    with contextlib.ExitStack() as ctx:
        sb = lambda n, sh, dt: ctx.enter_context(nc.sbuf_tensor(n, sh, dt)).ap()
        c = make_consts(s, ctx)
        pool = c["pool"]
        P0 = ctx.enter_context(nc.psum_tensor("P0", [128, 128], F32)).ap()
        PC = ctx.enter_context(nc.psum_tensor("PC", [1, NEXP], F32)).ap()
        PT = ctx.enter_context(nc.psum_tensor("PT", [128, 128], BF16)).ap()

        xin = [sb(f"xin{j}", [128, D], F32) for j in range(2)]
        xn3 = [sb(f"xn3_{i}", [128, D], BF16) for i in range(NT)]
        xn3R = [f"xn3_{i}" for i in range(NT)]
        xn3T = [sb(f"xn3T_{cc}", [128, TOK], BF16) for cc in range(DC)]
        xn3TR = [f"xn3T_{cc}" for cc in range(DC)]
        rwf = sb("rwf", [128, NEXP], F32)
        rwb = sb("rwb", [128, NEXP * DC], BF16)
        rbr = sb("rbr", [1, NEXP], F32)
        rb_bc = sb("rb_bc", [128, NEXP], F32)
        iota_bc = sb("iota_bc", [128, NEXP], F32)
        iota_r = sb("iota_r", [1, NEXP], F32)
        TS = sb("TS", [128, 8 * NEXP + 16], F32)
        lg = sb("lg", [128, NEXP], F32)
        eqs = sb("eqs", [128, NEXP], F32)
        st2 = sb("st2", [128, 4], F32)

        lns = LNStream(s, ctx, "ln3", pool)
        for i in range(NT):
            j = i % 2
            s.dma(xin[j], x3[i * 128:(i + 1) * 128, :], writes=[f"xin{j}"])
            lns.tile(i, xin[j], f"xin{j}", [(xn3[i], xn3R[i])])
            s.dma(xn3_o[i * 128:(i + 1) * 128, :], xn3[i], reads=[xn3R[i]])
        emit_transpose_to_featmajor(s, "tr3", xn3, xn3R, xn3T, xn3TR,
                                    PT, "PT", c["ident_bf"])

        for cc in range(DC):
            s.dma(rwf, rw[cc * 128:(cc + 1) * 128, :], writes=["rwf"])
            s.op("vector", lambda e, cc=cc: e.tensor_copy(
                out=rwb[:, cc * NEXP:(cc + 1) * NEXP], in_=rwf),
                reads=["rwf"], writes=["rwb"])
        s.dma(rbr, rb, writes=["rbr"])
        s.op("tensor", lambda e: e.matmul(P0[:, 0:NEXP], c["ones_row_f32"], rbr,
                                          start=True, stop=True),
             reads=["rbr", "c_ones_row_f32"], writes=["P0"])
        s.op("vector", lambda e: e.tensor_copy(out=rb_bc, in_=P0[:, 0:NEXP]),
             reads=["P0"], writes=["rb_bc"])
        s.op("gpsimd", lambda e: e.iota(iota_r, pattern=[[1, NEXP]], base=0,
                                        channel_multiplier=0,
                                        allow_small_or_imprecise_dtypes=True),
             writes=["iota_r"])
        s.op("tensor", lambda e: e.matmul(P0[:, 0:NEXP], c["ones_row_f32"], iota_r,
                                          start=True, stop=True),
             reads=["iota_r", "c_ones_row_f32"], writes=["P0"])
        s.op("vector", lambda e: e.tensor_copy(out=iota_bc, in_=P0[:, 0:NEXP]),
             reads=["P0"], writes=["iota_bc"])

        m1 = TS[:, 0:1]
        nm1 = TS[:, 1:2]
        m2 = TS[:, 2:3]
        zcol = TS[:, 3:4]
        p1 = TS[:, 4:5]
        p2 = TS[:, 5:6]
        s12 = TS[:, 6:7]
        r12 = TS[:, 7:8]
        em2 = TS[:, 8:9]
        eq1 = TS[:, NEXP:2 * NEXP]
        eq2 = TS[:, 2 * NEXP:3 * NEXP]
        lm = TS[:, 3 * NEXP:4 * NEXP]
        tmp16 = TS[:, 4 * NEXP:5 * NEXP]
        pe = TS[:, 5 * NEXP:6 * NEXP]

        for i in range(NT):
            for cc in range(DC):
                s.op("tensor", lambda e, i=i, cc=cc: e.matmul(
                    P0[:, 0:NEXP], xn3T[cc][:, i * 128:(i + 1) * 128],
                    rwb[:, cc * NEXP:(cc + 1) * NEXP],
                    start=(cc == 0), stop=(cc == DC - 1)),
                    reads=[xn3TR[cc], "rwb"], writes=["P0"])
            s.op("vector", lambda e: e.tensor_add(lg, P0[:, 0:NEXP], rb_bc),
                 reads=["P0", "rb_bc"], writes=["lg"])
            s.op("vector", lambda e: e.reduce_max(m1, lg, axis=AX.X),
                 reads=["lg"], writes=["TS"])
            s.op("vector", lambda e: e.tensor_single_scalar(
                out=eq1, in_=lg, scalar=m1, op=ALU.is_equal),
                reads=["lg", "TS"], writes=["TS"])
            s.op("vector", lambda e: e.scalar_tensor_tensor(
                out=lm, in0=eq1, scalar=-1e30, in1=lg, op0=ALU.mult, op1=ALU.add),
                reads=["TS", "lg"], writes=["TS"])
            s.op("vector", lambda e: e.reduce_max(m2, lm, axis=AX.X),
                 reads=["TS"], writes=["TS"])
            s.op("vector", lambda e: e.tensor_single_scalar(
                out=eq2, in_=lm, scalar=m2, op=ALU.is_equal),
                reads=["TS"], writes=["TS"])
            s.op("vector", lambda e: e.tensor_mul(tmp16, eq1, iota_bc),
                 reads=["TS", "iota_bc"], writes=["TS"])
            s.op("vector", lambda e: e.reduce_sum(st2[:, 0:1], tmp16, axis=AX.X),
                 reads=["TS"], writes=["st2"])
            s.op("vector", lambda e: e.tensor_mul(tmp16, eq2, iota_bc),
                 reads=["TS", "iota_bc"], writes=["TS"])
            s.op("vector", lambda e: e.reduce_sum(st2[:, 1:2], tmp16, axis=AX.X),
                 reads=["TS"], writes=["st2"])
            s.dma(e12_o[i * 128:(i + 1) * 128, :], st2[:, 0:2], reads=["st2"])
            s.op("vector", lambda e: e.tensor_scalar_mul(nm1, m1, -1.0),
                 reads=["TS"], writes=["TS"])
            s.op("scalar", lambda e: e.activation(
                out=pe, in_=lg, func=AF.Exp, bias=nm1, accum_out=zcol),
                reads=["lg", "TS"], writes=["TS"])
            s.op("vector", lambda e: e.reciprocal(out=p1, in_=zcol),
                 reads=["TS"], writes=["TS"])
            s.op("scalar", lambda e: e.activation(
                out=em2, in_=m2, func=AF.Exp, bias=nm1),
                reads=["TS"], writes=["TS"])
            s.op("vector", lambda e: e.tensor_mul(p2, em2, p1),
                 reads=["TS"], writes=["TS"])
            s.op("vector", lambda e: e.tensor_add(s12, p1, p2),
                 reads=["TS"], writes=["TS"])
            s.op("vector", lambda e: e.tensor_scalar_add(s12, s12, 1e-9),
                 reads=["TS"], writes=["TS"])
            s.op("vector", lambda e: e.reciprocal(out=r12, in_=s12),
                 reads=["TS"], writes=["TS"])
            s.op("vector", lambda e: e.tensor_mul(st2[:, 2:3], p1, r12),
                 reads=["TS"], writes=["st2"])
            s.op("vector", lambda e: e.tensor_mul(st2[:, 3:4], p2, r12),
                 reads=["TS"], writes=["st2"])
            s.dma(w12_o[i * 128:(i + 1) * 128, :], st2[:, 2:4], reads=["st2"])
            s.op("vector", lambda e: e.tensor_add(eqs, eq1, eq2),
                 reads=["TS"], writes=["eqs"])
            s.op("tensor", lambda e, i=i: e.matmul(
                PC, c["ones_col_f32"], eqs, start=(i == 0), stop=(i == NT - 1)),
                reads=["eqs", "c_ones_col_f32"], writes=["PC"])
        crow = sb("crow", [1, NEXP], F32)
        s.op("vector", lambda e: e.tensor_copy(out=crow, in_=PC),
             reads=["PC"], writes=["crow"])
        s.dma(cnt_o, crow, reads=["crow"])
        s.finalize(ctx)
    return nc



def build_launch_E2(upto=4, do_scatter=True, do_zero=True, ep_parts=7, zero_q="gpsimd"):
    nc = bass.Bass("TRN2", target_bir_lowering=False, debug=False)
    cnts = nc.dram_tensor("cnts", [8, NEXP], F32, kind="ExternalInput").ap()
    pmask = nc.dram_tensor("pmask", [8, 1], F32, kind="ExternalInput").ap()
    e12 = nc.dram_tensor("e12", [TOK, 2], F32, kind="ExternalInput").ap()
    w12 = nc.dram_tensor("w12", [TOK, 2], F32, kind="ExternalInput").ap()
    tokids = nc.dram_tensor("tokids", [TOK, 1], F32, kind="ExternalInput").ap()
    tbl_o = nc.dram_tensor("tbl", [NEXP * CAP, 1], F32, kind="ExternalOutput").ap()
    gidx_o = nc.dram_tensor("gidx", [TOK, 2], F32, kind="ExternalOutput").ap()
    wk_o = nc.dram_tensor("wk", [TOK, 2], F32, kind="ExternalOutput").ap()

    s = Sched(nc)
    with contextlib.ExitStack() as ctx:
        sb = lambda n, sh, dt: ctx.enter_context(nc.sbuf_tensor(n, sh, dt)).ap()
        c = make_consts(s, ctx, need_identity_f32=True)
        PTB = ctx.enter_context(nc.psum_tensor("PTB", [128, 128], BF16)).ap()
        PTF = ctx.enter_context(nc.psum_tensor("PTF", [128, 128], F32)).ap()
        PR = ctx.enter_context(nc.psum_tensor("PR", [16, 512], F32)).ap()
        PRow = ctx.enter_context(nc.psum_tensor("PRow", [1, 512], F32)).ap()

        cnt_sb = sb("cnt_sb", [8, NEXP], F32)
        pm_sb = sb("pm_sb", [8, 1], F32)
        base_r = sb("base_r", [1, NEXP], F32)
        base_c = sb("base_c", [NEXP, 1], F32)
        e12t = sb("e12t", [128, 2], F32)
        e12b = sb("e12b", [128, 2], BF16)
        idxT = sb("idxT", [2, TOK], BF16)
        U2 = sb("U2", [2, 2 * NEXP], BF16)
        iota16 = sb("iota16", [NEXP, 1], F32)
        e2560 = sb("e2560", [NEXP, 1], F32)
        ib = sb("ib", [NEXP, TOK], F32)
        ind0 = sb("ind0", [NEXP, TOK], F32)
        ind1 = sb("ind1", [NEXP, TOK], F32)
        S01 = sb("S01", [NEXP, TOK], F32)
        P01 = sb("P01", [NEXP, TOK], F32)
        zer = sb("zer", [NEXP, TOK], F32)
        Eex = sb("Eex", [NEXP, TOK], F32)
        posk = sb("posk", [NEXP, TOK], F32)
        wm = sb("wm", [NEXP, TOK], F32)
        gm = sb("gm", [NEXP, TOK], F32)
        growk = [sb(f"grow{k}", [1, TOK], F32) for k in range(2)]
        wrowk = [sb(f"wrow{k}", [1, TOK], F32) for k in range(2)]
        w12t = sb("w12t", [128, 2], F32)
        tid = sb("tid", [128, 1], F32)
        gcol = sb("gcol", [128, 2], F32)
        wcol = sb("wcol", [128, 2], F32)
        icol = sb("icol", [128, 1], I32)
        ztile = sb("ztile", [128, NEXP * CAP // 128], F32)

        s.dma(cnt_sb, cnts, writes=["cnt_sb"])
        s.dma(pm_sb, pmask, writes=["pm_sb"])
        s.op("tensor", lambda e: e.matmul(PRow[0:1, 0:NEXP], pm_sb, cnt_sb,
                                          start=True, stop=True),
             reads=["cnt_sb", "pm_sb"], writes=["PRow"])
        s.op("vector", lambda e: e.tensor_copy(out=base_r, in_=PRow[0:1, 0:NEXP]),
             reads=["PRow"], writes=["base_r"])
        s.op("tensor", lambda e: e.transpose(
            PTF[0:NEXP, 0:1], base_r, c["ident_f32"][0:1, 0:1]),
            reads=["base_r", "c_ident_f32"], writes=["PTF"])
        s.op("vector", lambda e: e.tensor_copy(out=base_c, in_=PTF[0:NEXP, 0:1]),
             reads=["PTF"], writes=["base_c"])
        s.op("gpsimd", lambda e: e.iota(iota16, pattern=[[0, 1]], base=0,
                                        channel_multiplier=1,
                                        allow_small_or_imprecise_dtypes=True),
             writes=["iota16"])
        s.op("vector", lambda e: e.tensor_scalar_mul(e2560, iota16, float(CAP)),
             reads=["iota16"], writes=["e2560"])
        s.op("gpsimd", lambda e: e.memset(U2, 0.0), writes=["U2"])
        s.op("gpsimd", lambda e: e.affine_select(
            out=U2, in_=U2, compare_op=ALU.not_equal, fill=1.0, base=0,
            pattern=[[-1, 2], [0, NEXP]], channel_multiplier=1),
            reads=["U2"], writes=["U2"])
        s.op("gpsimd", lambda e: e.memset(zer, 0.0), writes=["zer"])

        for i in range(NT):
            s.dma(e12t, e12[i * 128:(i + 1) * 128, :], writes=["e12t"])
            s.op("vector", lambda e: e.tensor_copy(out=e12b, in_=e12t),
                 reads=["e12t"], writes=["e12b"])
            s.op("tensor", lambda e: e.transpose(
                PTB[0:2, 0:128], e12b, c["ident_bf"]),
                reads=["e12b", "c_ident_bf"], writes=["PTB"])
            s.op("vector", lambda e, i=i: e.tensor_copy(
                out=idxT[:, i * 128:(i + 1) * 128], in_=PTB[0:2, 0:128]),
                reads=["PTB"], writes=["idxT"])

        if upto < 2:
            s.finalize(ctx)
            return nc
        for k, indk in ((0, ind0), (1, ind1)):
            for ch in range(4):
                s.op("tensor", lambda e, k=k, ch=ch: e.matmul(
                    PR, U2[:, k * NEXP:(k + 1) * NEXP],
                    idxT[:, ch * 512:(ch + 1) * 512], start=True, stop=True),
                    reads=["U2", "idxT"], writes=["PR"])
                s.op("vector", lambda e, ch=ch: e.tensor_copy(
                    out=ib[:, ch * 512:(ch + 1) * 512], in_=PR),
                    reads=["PR"], writes=["ib"])
            s.op("vector", lambda e, indk=indk: e.tensor_single_scalar(
                out=indk, in_=ib, scalar=iota16, op=ALU.is_equal),
                reads=["ib", "iota16"], writes=["ind0" if indk is ind0 else "ind1"])

        s.op("vector", lambda e: e.tensor_add(S01, ind0, ind1),
             reads=["ind0", "ind1"], writes=["S01"])
        s.op("vector", lambda e: e.tensor_tensor_scan(
            out=P01, data0=S01, data1=zer, initial=0.0,
            op0=ALU.add, op1=ALU.add),
            reads=["S01", "zer"], writes=["P01"])
        s.op("vector", lambda e: e.tensor_sub(Eex, P01, S01),
             reads=["P01", "S01"], writes=["Eex"])

        if upto < 3:
            s.finalize(ctx)
            return nc
        for k, indk, indkR in ((0, ind0, "ind0"), (1, ind1, "ind1")):
            if k == 0:
                s.op("vector", lambda e: e.tensor_scalar_add(posk, Eex, base_c),
                     reads=["Eex", "base_c"], writes=["posk"])
            else:
                s.op("vector", lambda e: e.tensor_add(posk, Eex, ind0),
                     reads=["Eex", "ind0"], writes=["posk"])
                s.op("vector", lambda e: e.tensor_scalar_add(posk, posk, base_c),
                     reads=["posk", "base_c"], writes=["posk"])
            s.op("vector", lambda e: e.tensor_single_scalar(
                out=wm, in_=posk, scalar=float(CAP), op=ALU.is_lt),
                reads=["posk"], writes=["wm"])
            s.op("vector", lambda e, indk=indk, indkR=indkR: e.tensor_mul(
                wm, wm, indk), reads=["wm", indkR], writes=["wm"])
            s.op("vector", lambda e: e.tensor_scalar_add(gm, posk, e2560),
                 reads=["posk", "e2560"], writes=["gm"])
            s.op("vector", lambda e, indk=indk, indkR=indkR: e.tensor_mul(
                gm, gm, indk), reads=["gm", indkR], writes=["gm"])
            for ch in range(4):
                s.op("tensor", lambda e, ch=ch: e.matmul(
                    PRow, c["ones_col_f32"][0:NEXP, :],
                    gm[:, ch * 512:(ch + 1) * 512], start=True, stop=True),
                    reads=["gm", "c_ones_col_f32"], writes=["PRow"])
                s.op("vector", lambda e, k=k, ch=ch: e.tensor_copy(
                    out=growk[k][:, ch * 512:(ch + 1) * 512], in_=PRow),
                    reads=["PRow"], writes=[f"grow{k}"])
                s.op("tensor", lambda e, ch=ch: e.matmul(
                    PRow, c["ones_col_f32"][0:NEXP, :],
                    wm[:, ch * 512:(ch + 1) * 512], start=True, stop=True),
                    reads=["wm", "c_ones_col_f32"], writes=["PRow"])
                s.op("vector", lambda e, k=k, ch=ch: e.tensor_copy(
                    out=wrowk[k][:, ch * 512:(ch + 1) * 512], in_=PRow),
                    reads=["PRow"], writes=[f"wrow{k}"])
            s.op("vector", lambda e, k=k: e.scalar_tensor_tensor(
                out=growk[k], in0=wrowk[k], scalar=-16777216.0,
                in1=growk[k], op0=ALU.mult, op1=ALU.add),
                reads=[f"wrow{k}", f"grow{k}"], writes=[f"grow{k}"])
            s.op("vector", lambda e, k=k: e.tensor_scalar_add(
                growk[k], growk[k], 16777216.0),
                reads=[f"grow{k}"], writes=[f"grow{k}"])

        if upto < 4:
            s.finalize(ctx)
            return nc
        if do_zero:
            s.op("gpsimd", lambda e: e.memset(ztile, 0.0), writes=["ztile"])
            s.dma(tbl_o.rearrange("(a b) c -> a (b c)", a=128), ztile,
                  reads=["ztile"], writes=["tbl"], queue=zero_q)

        for i in range(NT):
            if ep_parts >= 1:
                for k in range(2):
                    s.op("tensor", lambda e, i=i, k=k: e.transpose(
                        PTF[:, 0:1], growk[k][:, i * 128:(i + 1) * 128],
                        c["ident_f32"][0:1, 0:1]),
                        reads=[f"grow{k}", "c_ident_f32"], writes=["PTF"])
                    s.op("vector", lambda e, k=k: e.tensor_copy(
                        out=gcol[:, k:k + 1], in_=PTF[:, 0:1]),
                        reads=["PTF"], writes=["gcol"])
                    s.op("tensor", lambda e, i=i, k=k: e.transpose(
                        PTF[:, 0:1], wrowk[k][:, i * 128:(i + 1) * 128],
                        c["ident_f32"][0:1, 0:1]),
                        reads=[f"wrow{k}", "c_ident_f32"], writes=["PTF"])
                    s.op("vector", lambda e, k=k: e.tensor_copy(
                        out=wcol[:, k:k + 1], in_=PTF[:, 0:1]),
                        reads=["PTF"], writes=["wcol"])
            if ep_parts >= 2:
                s.dma(gidx_o[i * 128:(i + 1) * 128, :], gcol, reads=["gcol"])
            if ep_parts >= 3:
                s.dma(w12t, w12[i * 128:(i + 1) * 128, :], writes=["w12t"])
                s.op("vector", lambda e: e.tensor_mul(wcol, wcol, w12t),
                     reads=["wcol", "w12t"], writes=["wcol"])
                s.dma(wk_o[i * 128:(i + 1) * 128, :], wcol, reads=["wcol"])
            if not do_scatter:
                continue
            s.dma(tid, tokids[i * 128:(i + 1) * 128, :], writes=["tid"])
            for k in range(2):
                s.op("vector", lambda e, k=k: e.tensor_copy(
                    out=icol, in_=gcol[:, k:k + 1]),
                    reads=["gcol"], writes=["icol"])
                s.op("gpsimd", lambda e: e.indirect_dma_start(
                    out=tbl_o, out_offset=IndirectOffsetOnAxis(ap=icol, axis=0),
                    in_=tid, in_offset=None),
                    reads=["icol", "tid", "tbl"], writes=["tbl"], inc=16,
                    sem_key="dma:tbl")
        s.finalize(ctx)
    return nc



FEXP = 1408
FC = FEXP // 128


def build_launch_E3():
    nc = bass.Bass("TRN2", target_bir_lowering=False, debug=False)
    xall = nc.dram_tensor("xall", [8 * TOK, D], BF16, kind="ExternalInput").ap()
    tbl2 = nc.dram_tensor("tbl2", [8, 128, 2 * CAP // 128], F32, kind="ExternalInput").ap()
    wg = nc.dram_tensor("wg", [2, D, FEXP], F32, kind="ExternalInput").ap()
    wu = nc.dram_tensor("wu", [2, D, FEXP], F32, kind="ExternalInput").ap()
    wd = nc.dram_tensor("wd", [2, FEXP, D], F32, kind="ExternalInput").ap()
    eout = nc.dram_tensor("eout", [2 * CAP, D], BF16, kind="ExternalOutput").ap()

    s = Sched(nc)
    with contextlib.ExitStack() as ctx:
        sb = lambda n, sh, dt: ctx.enter_context(nc.sbuf_tensor(n, sh, dt)).ap()
        c = make_consts(s, ctx)
        PU = ctx.enter_context(nc.psum_tensor("PU", [128, 512], F32)).ap()
        PG = ctx.enter_context(nc.psum_tensor("PG", [128, 512], F32)).ap()
        PD = ctx.enter_context(nc.psum_tensor("PD", [128, 1024], F32)).ap()
        PT = ctx.enter_context(nc.psum_tensor("PT", [128, 128], BF16)).ap()

        NS = 2 * CAP // 128
        tsum = sb("tsum", [128, NS], F32)
        tpart = sb("tpart", [128, NS], F32)
        ti32 = sb("ti32", [128, NS], I32)
        xe = sb("xe", [128, D], BF16)
        xeT = [sb(f"xeT{cc}", [128, CAP], BF16) for cc in range(DC)]
        xeTR = [f"xeT{cc}" for cc in range(DC)]
        wsc = sb("wsc", [128, FEXP], F32)
        wgb = [sb(f"wgb{cc}", [128, FEXP], BF16) for cc in range(DC)]
        wub = [sb(f"wub{cc}", [128, FEXP], BF16) for cc in range(DC)]
        wdb = [sb(f"wdb{fc}", [128, D], BF16) for fc in range(FC)]
        gs = sb("gs", [128, 512], BF16)
        us = sb("us", [128, 512], BF16)
        hT = [sb(f"hT{fc}", [128, CAP], BF16) for fc in range(FC)]
        stg = [sb(f"stg{j}", [128, D], BF16) for j in range(2)]

        for r in range(8):
            dst = tpart if r > 0 else tsum
            dstR = "tpart" if r > 0 else "tsum"
            s.dma(dst, tbl2[r], writes=[dstR])
            if r > 0:
                s.op("vector", lambda e: e.tensor_add(tsum, tsum, tpart),
                     reads=["tsum", "tpart"], writes=["tsum"])
        s.op("vector", lambda e: e.tensor_copy(out=ti32, in_=tsum),
             reads=["tsum"], writes=["ti32"])

        for el in range(2):
            for cc in range(DC):
                s.dma(wsc, wg[el, cc * 128:(cc + 1) * 128, :], writes=["wsc"])
                s.op("vector", lambda e, cc=cc: e.tensor_copy(out=wgb[cc], in_=wsc),
                     reads=["wsc"], writes=[f"wgb{cc}"])
                s.dma(wsc, wu[el, cc * 128:(cc + 1) * 128, :], writes=["wsc"])
                s.op("vector", lambda e, cc=cc: e.tensor_copy(out=wub[cc], in_=wsc),
                     reads=["wsc"], writes=[f"wub{cc}"])
            for fc in range(FC):
                s.dma(wsc[:, 0:D], wd[el, fc * 128:(fc + 1) * 128, :], writes=["wsc"])
                s.op("vector", lambda e, fc=fc: e.tensor_copy(
                    out=wdb[fc], in_=wsc[:, 0:D]), reads=["wsc"], writes=[f"wdb{fc}"])
            for st_ in range(20):
                col = el * 20 + st_
                s.op("gpsimd", lambda e, col=col: e.indirect_dma_start(
                    out=xe, out_offset=None, in_=xall[0:128, :],
                    in_offset=IndirectOffsetOnAxis(ap=ti32[:, col:col + 1], axis=0)),
                    reads=["ti32"], writes=["xe"], inc=16, sem_key="dma:xe")
                for cc in range(DC):
                    s.op("tensor", lambda e, cc=cc: e.transpose(
                        PT, xe[:, cc * 128:(cc + 1) * 128], c["ident_bf"]),
                        reads=["xe", "c_ident_bf"], writes=["PT"])
                    s.op("vector", lambda e, cc=cc, st_=st_: e.tensor_copy(
                        out=xeT[cc][:, st_ * 128:(st_ + 1) * 128], in_=PT),
                        reads=["PT"], writes=[xeTR[cc]])
            for fc in range(FC):
                for sg in range(5):
                    ss = slice(sg * 512, (sg + 1) * 512)
                    for cc in range(DC):
                        s.op("tensor", lambda e, fc=fc, cc=cc, ss=ss: e.matmul(
                            PG, wgb[cc][:, fc * 128:(fc + 1) * 128],
                            xeT[cc][:, ss], start=(cc == 0), stop=(cc == DC - 1)),
                            reads=[f"wgb{cc}", xeTR[cc]], writes=["PG"])
                    for cc in range(DC):
                        s.op("tensor", lambda e, fc=fc, cc=cc, ss=ss: e.matmul(
                            PU, wub[cc][:, fc * 128:(fc + 1) * 128],
                            xeT[cc][:, ss], start=(cc == 0), stop=(cc == DC - 1)),
                            reads=[f"wub{cc}", xeTR[cc]], writes=["PU"])
                    s.op("scalar", lambda e: e.activation(
                        out=gs, in_=PG, func=AF.Silu),
                        reads=["PG"], writes=["gs"])
                    s.op("vector", lambda e: e.tensor_copy(out=us, in_=PU),
                         reads=["PU"], writes=["us"])
                    s.op("vector", lambda e, fc=fc, ss=ss: e.tensor_mul(
                        hT[fc][:, ss], gs, us),
                        reads=["gs", "us"], writes=[f"hT{fc}"])
            for st_ in range(20):
                for fc in range(FC):
                    mm_split(s, PD, "PD", hT[fc][:, st_ * 128:(st_ + 1) * 128],
                             wdb[fc], [f"hT{fc}", f"wdb{fc}"],
                             start=(fc == 0), stop=(fc == FC - 1))
                j = st_ % 2
                s.op("vector", lambda e, j=j: e.tensor_copy(out=stg[j], in_=PD),
                     reads=["PD"], writes=[f"stg{j}"])
                s.dma(eout[(el * 20 + st_) * 128:(el * 20 + st_ + 1) * 128, :],
                      stg[j], reads=[f"stg{j}"])
        s.finalize(ctx)
    return nc



def build_launch_E4(phases=15):
    nc = bass.Bass("TRN2", target_bir_lowering=False, debug=False)
    x3 = nc.dram_tensor("x3", [TOK, D], F32, kind="ExternalInput").ap()
    xn3_i = nc.dram_tensor("xn3", [TOK, D], BF16, kind="ExternalInput").ap()
    eall = nc.dram_tensor("eall", [NEXP * CAP, D], BF16, kind="ExternalInput").ap()
    gidx = nc.dram_tensor("gidx", [TOK, 2], F32, kind="ExternalInput").ap()
    wk = nc.dram_tensor("wk", [TOK, 2], F32, kind="ExternalInput").ap()
    swg = nc.dram_tensor("swg", [D, FEXP], F32, kind="ExternalInput").ap()
    swu = nc.dram_tensor("swu", [D, FEXP], F32, kind="ExternalInput").ap()
    swd = nc.dram_tensor("swd", [FEXP, D], F32, kind="ExternalInput").ap()
    x4_o = nc.dram_tensor("x4", [TOK, D], F32, kind="ExternalOutput").ap()

    s = Sched(nc)
    with contextlib.ExitStack() as ctx:
        sb = lambda n, sh, dt: ctx.enter_context(nc.sbuf_tensor(n, sh, dt)).ap()
        c = make_consts(s, ctx)
        PU = ctx.enter_context(nc.psum_tensor("PU", [128, 512], F32)).ap()
        PG = ctx.enter_context(nc.psum_tensor("PG", [128, 512], F32)).ap()
        PD = ctx.enter_context(nc.psum_tensor("PD", [128, 1024], F32)).ap()
        PT = ctx.enter_context(nc.psum_tensor("PT", [128, 128], BF16)).ap()

        xn3 = [sb(f"xn3_{i}", [128, D], BF16) for i in range(NT)]
        xn3R = [f"xn3_{i}" for i in range(NT)]
        xn3T = [sb(f"xn3T_{cc}", [128, TOK], BF16) for cc in range(DC)]
        xn3TR = [f"xn3T_{cc}" for cc in range(DC)]
        wsc = sb("wsc", [128, FEXP], F32)
        wgb = [sb(f"wgb{cc}", [128, FEXP], BF16) for cc in range(DC)]
        wub = [sb(f"wub{cc}", [128, FEXP], BF16) for cc in range(DC)]
        wdb = [sb(f"wdb{fc}", [128, D], BF16) for fc in range(FC)]
        gs = sb("gs", [128, 512], BF16)
        us = sb("us", [128, 512], BF16)
        hT = [sb(f"hT{fc}", [128, TOK], BF16) for fc in range(FC)]
        gcol = sb("gcol", [128, 2], F32)
        icol = sb("icol", [128, 1], I32)
        wkt = sb("wkt", [128, 2], F32)
        ge = sb("ge", [128, D], BF16)
        acc = sb("acc", [128, D], F32)
        x3t = sb("x3t", [128, D], F32)
        stg = sb("stg", [128, D], F32)

        s.op("gpsimd", lambda e: e.memset(ge, 0.0), writes=["ge"])
        for i in range(NT):
            s.dma(xn3[i], xn3_i[i * 128:(i + 1) * 128, :], writes=[xn3R[i]])
        emit_transpose_to_featmajor(s, "tr4", xn3, xn3R, xn3T, xn3TR,
                                    PT, "PT", c["ident_bf"])
        for cc in range(DC):
            s.dma(wsc, swg[cc * 128:(cc + 1) * 128, :], writes=["wsc"])
            s.op("vector", lambda e, cc=cc: e.tensor_copy(out=wgb[cc], in_=wsc),
                 reads=["wsc"], writes=[f"wgb{cc}"])
            s.dma(wsc, swu[cc * 128:(cc + 1) * 128, :], writes=["wsc"])
            s.op("vector", lambda e, cc=cc: e.tensor_copy(out=wub[cc], in_=wsc),
                 reads=["wsc"], writes=[f"wub{cc}"])
        for fc in range(FC):
            s.dma(wsc[:, 0:D], swd[fc * 128:(fc + 1) * 128, :], writes=["wsc"])
            s.op("vector", lambda e, fc=fc: e.tensor_copy(
                out=wdb[fc], in_=wsc[:, 0:D]), reads=["wsc"], writes=[f"wdb{fc}"])
        if not (phases & 2):
            s.finalize(ctx)
            return nc
        for fc in range(FC):
            for tg in range(4):
                ss = slice(tg * 512, (tg + 1) * 512)
                for cc in range(DC):
                    s.op("tensor", lambda e, fc=fc, cc=cc, ss=ss: e.matmul(
                        PG, wgb[cc][:, fc * 128:(fc + 1) * 128],
                        xn3T[cc][:, ss], start=(cc == 0), stop=(cc == DC - 1)),
                        reads=[f"wgb{cc}", xn3TR[cc]], writes=["PG"])
                for cc in range(DC):
                    s.op("tensor", lambda e, fc=fc, cc=cc, ss=ss: e.matmul(
                        PU, wub[cc][:, fc * 128:(fc + 1) * 128],
                        xn3T[cc][:, ss], start=(cc == 0), stop=(cc == DC - 1)),
                        reads=[f"wub{cc}", xn3TR[cc]], writes=["PU"])
                s.op("scalar", lambda e: e.activation(
                    out=gs, in_=PG, func=AF.Silu), reads=["PG"], writes=["gs"])
                s.op("vector", lambda e: e.tensor_copy(out=us, in_=PU),
                     reads=["PU"], writes=["us"])
                s.op("vector", lambda e, fc=fc, ss=ss: e.tensor_mul(
                    hT[fc][:, ss], gs, us), reads=["gs", "us"], writes=[f"hT{fc}"])
        if not (phases & 4):
            s.finalize(ctx)
            return nc
        for i in range(NT):
            s.dma(gcol, gidx[i * 128:(i + 1) * 128, :], writes=["gcol"])
            s.dma(wkt, wk[i * 128:(i + 1) * 128, :], writes=["wkt"])
            for k in range(2):
                s.op("vector", lambda e, k=k: e.tensor_copy(
                    out=icol, in_=gcol[:, k:k + 1]), reads=["gcol"], writes=["icol"])
                s.op("gpsimd", lambda e: e.indirect_dma_start(
                    out=ge, out_offset=None, in_=eall[0:128, :],
                    in_offset=IndirectOffsetOnAxis(ap=icol, axis=0),
                    bounds_check=NEXP * CAP, oob_is_err=False),
                    reads=["icol"], writes=["ge"], inc=16, sem_key="dma:ge")
                if k == 0:
                    s.op("vector", lambda e: e.tensor_single_scalar(
                        out=acc, in_=ge, scalar=wkt[:, 0:1], op=ALU.mult),
                        reads=["ge", "wkt"], writes=["acc"])
                else:
                    s.op("vector", lambda e: e.scalar_tensor_tensor(
                        out=acc, in0=ge, scalar=wkt[:, 1:2], in1=acc,
                        op0=ALU.mult, op1=ALU.add),
                        reads=["ge", "wkt", "acc"], writes=["acc"])
            if not (phases & 8):
                continue
            for dh in range(2):
                for fc in range(FC):
                    s.op("tensor", lambda e, i=i, dh=dh, fc=fc: e.matmul(
                        PD[:, dh * 512:(dh + 1) * 512],
                        hT[fc][:, i * 128:(i + 1) * 128],
                        wdb[fc][:, dh * 512:(dh + 1) * 512],
                        start=(fc == 0), stop=(fc == FC - 1)),
                        reads=[f"hT{fc}", f"wdb{fc}"], writes=["PD"])
            s.dma(x3t, x3[i * 128:(i + 1) * 128, :], writes=["x3t"])
            s.op("vector", lambda e: e.tensor_add(stg, PD, acc),
                 reads=["PD", "acc"], writes=["stg"])
            s.op("vector", lambda e: e.tensor_add(stg, stg, x3t),
                 reads=["stg", "x3t"], writes=["stg"])
            s.op("vector", lambda e, i=i: e.tensor_add(stg, stg, xn3[i]),
                 reads=["stg", xn3R[i]], writes=["stg"])
            s.dma(x4_o[i * 128:(i + 1) * 128, :], stg, reads=["stg"])
        s.finalize(ctx)
    return nc


def run_E(x3_full, inputs):
    nc1 = _get_prog("E1", build_launch_E1)
    rw = np.ascontiguousarray(inputs["L1_ffn_router_w"])
    rb = np.ascontiguousarray(inputs["L1_ffn_router_b"].reshape(1, NEXP))
    in_maps = [{"x3": np.ascontiguousarray(x3_full[i * TOK:(i + 1) * TOK]),
                "rw": rw, "rb": rb} for i in range(NCORES)]
    r1 = run_bass_kernel_spmd(nc1, in_maps, core_ids=list(range(NCORES)))
    cnts = np.stack([r1.results[i]["cnt"][0] for i in range(NCORES)])
    xn3_full = np.concatenate([r1.results[i]["xn3"] for i in range(NCORES)], 0)
    nc2 = _get_prog("E2", build_launch_E2)
    in_maps = []
    for ci in range(NCORES):
        pm = np.zeros((8, 1), np.float32)
        pm[:ci] = 1.0
        in_maps.append({
            "cnts": cnts, "pmask": pm,
            "e12": r1.results[ci]["e12"], "w12": r1.results[ci]["w12"],
            "tokids": np.arange(ci * TOK, (ci + 1) * TOK,
                                dtype=np.float32).reshape(TOK, 1),
        })
    r2 = run_bass_kernel_spmd(nc2, in_maps, core_ids=list(range(NCORES)))
    tbls = np.stack([r2.results[i]["tbl"][:, 0] for i in range(NCORES)])
    nc3 = _get_prog("E3", build_launch_E3)
    in_maps = []
    for ci in range(NCORES):
        sl = slice(2 * ci * CAP, 2 * (ci + 1) * CAP)
        in_maps.append({
            "xall": xn3_full,
            "tbl2": np.ascontiguousarray(
                tbls[:, sl].reshape(8, 2 * CAP // 128, 128).transpose(0, 2, 1)),
            "wg": np.ascontiguousarray(inputs["L1_ffn_w_gate"][2 * ci:2 * ci + 2]),
            "wu": np.ascontiguousarray(inputs["L1_ffn_w_up"][2 * ci:2 * ci + 2]),
            "wd": np.ascontiguousarray(inputs["L1_ffn_w_down"][2 * ci:2 * ci + 2]),
        })
    r3 = run_bass_kernel_spmd(nc3, in_maps, core_ids=list(range(NCORES)))
    eall = np.concatenate([r3.results[i]["eout"] for i in range(NCORES)], 0)
    nc4 = _get_prog("E4", build_launch_E4)
    swg = np.ascontiguousarray(inputs["L1_ffn_sw_gate"])
    swu = np.ascontiguousarray(inputs["L1_ffn_sw_up"])
    swd = np.ascontiguousarray(inputs["L1_ffn_sw_down"])
    in_maps = []
    for ci in range(NCORES):
        in_maps.append({
            "x3": np.ascontiguousarray(x3_full[ci * TOK:(ci + 1) * TOK]),
            "xn3": np.ascontiguousarray(xn3_full[ci * TOK:(ci + 1) * TOK]),
            "eall": eall,
            "gidx": r2.results[ci]["gidx"], "wk": r2.results[ci]["wk"],
            "swg": swg, "swu": swu, "swd": swd,
        })
    r4 = run_bass_kernel_spmd(nc4, in_maps, core_ids=list(range(NCORES)))
    return np.concatenate([r4.results[i]["x4"] for i in range(NCORES)], 0)



def build_launch_F():
    nc = bass.Bass("TRN2", target_bir_lowering=False, debug=False)
    x4 = nc.dram_tensor("x4", [TOK, D], F32, kind="ExternalInput").ap()
    ipw = nc.dram_tensor("ipw", [D, 2 * D], F32, kind="ExternalInput").ap()
    xinT_o = nc.dram_tensor("xinT", [D, TOK], BF16, kind="ExternalOutput").ap()
    zT_o = nc.dram_tensor("zT", [D, TOK], BF16, kind="ExternalOutput").ap()

    s = Sched(nc)
    with contextlib.ExitStack() as ctx:
        sb = lambda n, sh, dt: ctx.enter_context(nc.sbuf_tensor(n, sh, dt)).ap()
        c = make_consts(s, ctx)
        pool = c["pool"]
        P0 = ctx.enter_context(nc.psum_tensor("P0", [128, 512], F32)).ap()
        PT = ctx.enter_context(nc.psum_tensor("PT", [128, 128], BF16)).ap()
        xin = [sb(f"xin{j}", [128, D], F32) for j in range(2)]
        xn4 = [sb(f"xn4_{i}", [128, D], BF16) for i in range(NT)]
        xn4R = [f"xn4_{i}" for i in range(NT)]
        xn4T = [sb(f"xn4T_{cc}", [128, TOK], BF16) for cc in range(DC)]
        xn4TR = [f"xn4T_{cc}" for cc in range(DC)]
        wsc = sb("wsc", [128, 2 * D], F32)
        wbf = [sb(f"wbf{cc}", [128, 2 * D], BF16) for cc in range(DC)]
        stg = [sb(f"stg{j}", [128, 512], BF16) for j in range(2)]

        lns = LNStream(s, ctx, "ln4", pool)
        for i in range(NT):
            j = i % 2
            s.dma(xin[j], x4[i * 128:(i + 1) * 128, :], writes=[f"xin{j}"])
            lns.tile(i, xin[j], f"xin{j}", [(xn4[i], xn4R[i])])
        emit_transpose_to_featmajor(s, "trF", xn4, xn4R, xn4T, xn4TR,
                                    PT, "PT", c["ident_bf"])
        for cc in range(DC):
            s.dma(wsc, ipw[cc * 128:(cc + 1) * 128, :], writes=["wsc"])
            s.op("vector", lambda e, cc=cc: e.tensor_copy(out=wbf[cc], in_=wsc),
                 reads=["wsc"], writes=[f"wbf{cc}"])
        for eb in range(16):
            dst = xinT_o if eb < 8 else zT_o
            row = (eb % 8) * 128
            for tg in range(4):
                for cc in range(DC):
                    s.op("tensor", lambda e, eb=eb, tg=tg, cc=cc: e.matmul(
                        P0, wbf[cc][:, eb * 128:(eb + 1) * 128],
                        xn4T[cc][:, tg * 512:(tg + 1) * 512],
                        start=(cc == 0), stop=(cc == DC - 1)),
                        reads=[f"wbf{cc}", xn4TR[cc]], writes=["P0"])
                j = (eb * 4 + tg) % 2
                s.op("vector", lambda e, j=j: e.tensor_copy(out=stg[j], in_=P0),
                     reads=["P0"], writes=[f"stg{j}"])
                s.dma(dst[row:row + 128, tg * 512:(tg + 1) * 512], stg[j],
                      reads=[f"stg{j}"])
        s.finalize(ctx)
    return nc



def build_launch_G():
    nc = bass.Bass("TRN2", target_bir_lowering=False, debug=False)
    T4 = 4096
    xinT = nc.dram_tensor("xinT", [512, T4], BF16, kind="ExternalInput").ap()
    cw = nc.dram_tensor("cw", [512, 4], F32, kind="ExternalInput").ap()
    cb = nc.dram_tensor("cb", [512, 1], F32, kind="ExternalInput").ap()
    xpw = nc.dram_tensor("xpw", [512, 33], F32, kind="ExternalInput").ap()
    xcT_o = nc.dram_tensor("xcT", [512, T4], BF16, kind="ExternalOutput").ap()
    part_o = nc.dram_tensor("part", [33, T4], F32, kind="ExternalOutput").ap()

    s = Sched(nc)
    with contextlib.ExitStack() as ctx:
        sb = lambda n, sh, dt: ctx.enter_context(nc.sbuf_tensor(n, sh, dt)).ap()
        c = make_consts(s, ctx)
        PS = ctx.enter_context(nc.psum_tensor("PS", [33, 512], F32)).ap()
        xt = sb("xt", [128, T4], BF16)
        xp = sb("xp", [128, T4 + 3], F32)
        acc = sb("acc", [128, T4], F32)
        cwt = sb("cwt", [128, 4], F32)
        cbt = sb("cbt", [128, 1], F32)
        xcT = [sb(f"xcT{db}", [128, T4], BF16) for db in range(4)]
        xpw_b = [sb(f"xpw{db}", [128, 33], BF16) for db in range(4)]
        xpw_f = sb("xpw_f", [128, 33], F32)
        prt = sb("prt", [33, T4], F32)

        for db in range(4):
            r = db * 128
            s.dma(xt, xinT[r:r + 128, :], writes=["xt"])
            s.dma(cwt, cw[r:r + 128, :], writes=["cwt"])
            s.dma(cbt, cb[r:r + 128, :], writes=["cbt"])
            s.op("gpsimd", lambda e: e.memset(xp[:, 0:3], 0.0), writes=["xp"])
            s.op("vector", lambda e: e.tensor_copy(out=xp[:, 3:T4 + 3], in_=xt),
                 reads=["xt"], writes=["xp"])
            s.op("vector", lambda e: e.tensor_single_scalar(
                out=acc, in_=xp[:, 0:T4], scalar=cwt[:, 0:1], op=ALU.mult),
                reads=["xp", "cwt"], writes=["acc"])
            for jj in range(1, 4):
                s.op("vector", lambda e, jj=jj: e.scalar_tensor_tensor(
                    out=acc, in0=xp[:, jj:T4 + jj], scalar=cwt[:, jj:jj + 1],
                    in1=acc, op0=ALU.mult, op1=ALU.add),
                    reads=["xp", "cwt", "acc"], writes=["acc"])
            s.op("scalar", lambda e, db=db: e.activation(
                out=xcT[db], in_=acc, func=AF.Silu, bias=cbt),
                reads=["acc", "cbt"], writes=[f"xcT{db}"])
            s.dma(xcT_o[r:r + 128, :], xcT[db], reads=[f"xcT{db}"])
            s.dma(xpw_f, xpw[r:r + 128, :], writes=["xpw_f"])
            s.op("vector", lambda e, db=db: e.tensor_copy(out=xpw_b[db], in_=xpw_f),
                 reads=["xpw_f"], writes=[f"xpw{db}"])
        for tg in range(8):
            for db in range(4):
                s.op("tensor", lambda e, tg=tg, db=db: e.matmul(
                    PS, xpw_b[db], xcT[db][:, tg * 512:(tg + 1) * 512],
                    start=(db == 0), stop=(db == 3)),
                    reads=[f"xpw{db}", f"xcT{db}"], writes=["PS"])
            s.op("vector", lambda e, tg=tg: e.tensor_copy(
                out=prt[:, tg * 512:(tg + 1) * 512], in_=PS),
                reads=["PS"], writes=["prt"])
        s.dma(part_o, prt, reads=["prt"], queue="gpsimd")
        s.finalize(ctx)
    return nc



def build_launch_H():
    nc = bass.Bass("TRN2", target_bir_lowering=False, debug=False)
    T4 = 4096
    CH = 1024
    xcT = nc.dram_tensor("xcT", [512, T4], BF16, kind="ExternalInput").ap()
    pa = nc.dram_tensor("pa", [33, T4], F32, kind="ExternalInput").ap()
    pb = nc.dram_tensor("pb", [33, T4], F32, kind="ExternalInput").ap()
    alog = nc.dram_tensor("alog", [512, 16], F32, kind="ExternalInput").ap()
    dtw = nc.dram_tensor("dtw", [512, 1], F32, kind="ExternalInput").ap()
    dtb = nc.dram_tensor("dtb", [512, 1], F32, kind="ExternalInput").ap()
    yT_o = nc.dram_tensor("yT", [512, T4], BF16, kind="ExternalOutput").ap()

    s = Sched(nc)
    with contextlib.ExitStack() as ctx:
        sb = lambda n, sh, dt: ctx.enter_context(nc.sbuf_tensor(n, sh, dt)).ap()
        c = make_consts(s, ctx, need_identity_bf=False)
        PB = ctx.enter_context(nc.psum_tensor("PB", [128, 512], F32)).ap()

        ssma = sb("ssma", [33, T4], F32)
        ssmb = sb("ssmb", [33, 512], F32)
        C16 = sb("C16", [16, T4], F32)
        s33 = sb("s33", [1, T4], F32)
        U16 = sb("U16", [16, 16 * 128], F32)
        A_sb = [sb(f"A{db}", [128, 16], F32) for db in range(4)]
        atmp = sb("atmp", [128, 16], F32)
        dtwc = sb("dtwc", [128, 4], F32)
        dtbc = sb("dtbc", [128, 4], F32)
        s33bc = sb("s33bc", [128, CH], F32)
        dt_ = [sb(f"dt{db}", [128, CH], F32) for db in range(4)]
        dtx = [sb(f"dtx{db}", [128, CH], F32) for db in range(4)]
        u_ = sb("u_", [128, CH], F32)
        t1 = sb("t1", [128, CH], F32)
        t2 = sb("t2", [128, CH], F32)
        Bbc = sb("Bbc", [128, CH], F32)
        Cbc = sb("Cbc", [128, CH], F32)
        dA = sb("dA", [128, CH], F32)
        dBx = sb("dBx", [128, CH], F32)
        hh = sb("hh", [128, CH], F32)
        yac = [sb(f"yac{db}", [128, CH], F32) for db in range(4)]
        carry = [sb(f"carry{db}", [128, 16], F32) for db in range(4)]
        xct = sb("xct", [128, CH], BF16)
        ystg = sb("ystg", [128, CH], BF16)
        one_bias = c["pool"].get(1.0)

        s.dma(ssma, pa, writes=["ssma"])
        for ch8 in range(8):
            cs = slice(ch8 * 512, (ch8 + 1) * 512)
            s.dma(ssmb, pb[:, cs], writes=["ssmb"])
            s.op("vector", lambda e, cs=cs: e.tensor_add(
                ssma[:, cs], ssma[:, cs], ssmb),
                reads=["ssma", "ssmb"], writes=["ssma"])
        s.dma(C16, ssma[16:32, :], reads=["ssma"], writes=["C16"], queue="gpsimd")
        s.dma(s33, ssma[32:33, :], reads=["ssma"], writes=["s33"], queue="gpsimd")
        s.op("gpsimd", lambda e: e.memset(U16, 0.0), writes=["U16"])
        s.op("gpsimd", lambda e: e.affine_select(
            out=U16, in_=U16, compare_op=ALU.not_equal, fill=1.0, base=0,
            pattern=[[-1, 16], [0, 128]], channel_multiplier=1),
            reads=["U16"], writes=["U16"])
        for db in range(4):
            s.dma(atmp, alog[db * 128:(db + 1) * 128, :], writes=["atmp"])
            s.op("scalar", lambda e, db=db: e.activation(
                out=A_sb[db], in_=atmp, func=AF.Exp),
                reads=["atmp"], writes=[f"A{db}"])
            s.op("vector", lambda e, db=db: e.tensor_scalar_mul(
                A_sb[db], A_sb[db], -1.0), reads=[f"A{db}"], writes=[f"A{db}"])
            s.dma(dtwc[:, db:db + 1], dtw[db * 128:(db + 1) * 128, :], writes=["dtwc"])
            s.dma(dtbc[:, db:db + 1], dtb[db * 128:(db + 1) * 128, :], writes=["dtbc"])
            s.op("gpsimd", lambda e, db=db: e.memset(carry[db], 0.0),
                 writes=[f"carry{db}"])

        for tch in range(T4 // CH):
            ts0 = tch * CH
            for hq in range(CH // 512):
                s.op("tensor", lambda e, hq=hq, ts0=ts0: e.matmul(
                    PB, c["ones_row_f32"],
                    s33[:, ts0 + hq * 512:ts0 + (hq + 1) * 512],
                    start=True, stop=True),
                    reads=["s33", "c_ones_row_f32"], writes=["PB"])
                s.op("vector", lambda e, hq=hq: e.tensor_copy(
                    out=s33bc[:, hq * 512:(hq + 1) * 512], in_=PB),
                    reads=["PB"], writes=["s33bc"])
            for db in range(4):
                s.op("vector", lambda e, db=db: e.tensor_scalar(
                    out=u_, in0=s33bc, scalar1=dtwc[:, db:db + 1],
                    scalar2=dtbc[:, db:db + 1], op0=ALU.mult, op1=ALU.add),
                    reads=["s33bc", "dtwc", "dtbc"], writes=["u_"])
                s.op("scalar", lambda e: e.activation(out=t1, in_=u_, func=AF.Abs),
                     reads=["u_"], writes=["t1"])
                s.op("scalar", lambda e: e.activation(out=t2, in_=t1, func=AF.Exp,
                                                      scale=-1.0),
                     reads=["t1"], writes=["t2"])
                s.op("scalar", lambda e, db=db, b=one_bias: e.activation(
                    out=t1, in_=t2, func=AF.Ln, bias=b),
                    reads=["t2", "c_pool"], writes=["t1"])
                s.op("scalar", lambda e, db=db: e.activation(
                    out=dt_[db], in_=u_, func=AF.Relu),
                    reads=["u_"], writes=[f"dt{db}"])
                s.op("vector", lambda e, db=db: e.tensor_add(dt_[db], dt_[db], t1),
                     reads=[f"dt{db}", "t1"], writes=[f"dt{db}"])
                s.dma(xct, xcT[db * 128:(db + 1) * 128, ts0:ts0 + CH], writes=["xct"])
                s.op("vector", lambda e, db=db: e.tensor_mul(dtx[db], dt_[db], xct),
                     reads=[f"dt{db}", "xct"], writes=[f"dtx{db}"])
            for n in range(16):
                for hq in range(CH // 512):
                    s.op("tensor", lambda e, n=n, hq=hq, ts0=ts0: e.matmul(
                        PB, U16[:, n * 128:(n + 1) * 128],
                        ssma[0:16, ts0 + hq * 512:ts0 + (hq + 1) * 512],
                        start=True, stop=True),
                        reads=["U16", "ssma"], writes=["PB"])
                    s.op("vector", lambda e, hq=hq: e.tensor_copy(
                        out=Bbc[:, hq * 512:(hq + 1) * 512], in_=PB),
                        reads=["PB"], writes=["Bbc"])
                    s.op("tensor", lambda e, n=n, hq=hq, ts0=ts0: e.matmul(
                        PB, U16[:, n * 128:(n + 1) * 128],
                        C16[:, ts0 + hq * 512:ts0 + (hq + 1) * 512],
                        start=True, stop=True),
                        reads=["U16", "C16"], writes=["PB"])
                    s.op("vector", lambda e, hq=hq: e.tensor_copy(
                        out=Cbc[:, hq * 512:(hq + 1) * 512], in_=PB),
                        reads=["PB"], writes=["Cbc"])
                for db in range(4):
                    s.op("scalar", lambda e, db=db, n=n: e.activation(
                        out=dA, in_=dt_[db], func=AF.Exp,
                        scale=A_sb[db][:, n:n + 1]),
                        reads=[f"dt{db}", f"A{db}"], writes=["dA"])
                    s.op("vector", lambda e, db=db: e.tensor_mul(dBx, dtx[db], Bbc),
                         reads=[f"dtx{db}", "Bbc"], writes=["dBx"])
                    s.op("vector", lambda e, db=db, n=n: e.tensor_tensor_scan(
                        out=hh, data0=dA, data1=dBx,
                        initial=carry[db][:, n:n + 1],
                        op0=ALU.mult, op1=ALU.add),
                        reads=["dA", "dBx", f"carry{db}"], writes=["hh"])
                    s.op("vector", lambda e, db=db, n=n: e.tensor_copy(
                        out=carry[db][:, n:n + 1], in_=hh[:, CH - 1:CH]),
                        reads=["hh"], writes=[f"carry{db}"])
                    if n == 0:
                        s.op("vector", lambda e, db=db: e.tensor_mul(
                            yac[db], hh, Cbc),
                            reads=["hh", "Cbc"], writes=[f"yac{db}"])
                    else:
                        s.op("vector", lambda e, db=db: e.tensor_mul(t2, hh, Cbc),
                             reads=["hh", "Cbc"], writes=["t2"])
                        s.op("vector", lambda e, db=db: e.tensor_add(
                            yac[db], yac[db], t2),
                            reads=[f"yac{db}", "t2"], writes=[f"yac{db}"])
            for db in range(4):
                s.op("vector", lambda e, db=db: e.tensor_copy(out=ystg, in_=yac[db]),
                     reads=[f"yac{db}"], writes=["ystg"])
                s.dma(yT_o[db * 128:(db + 1) * 128, ts0:ts0 + CH], ystg,
                      reads=["ystg"])
        s.finalize(ctx)
    return nc



def build_launch_I():
    nc = bass.Bass("TRN2", target_bir_lowering=False, debug=False)
    yT = nc.dram_tensor("yT", [D, TOK], BF16, kind="ExternalInput").ap()
    zT = nc.dram_tensor("zT", [D, TOK], BF16, kind="ExternalInput").ap()
    opw = nc.dram_tensor("opw", [D, D], F32, kind="ExternalInput").ap()
    x4 = nc.dram_tensor("x4", [TOK, D], F32, kind="ExternalInput").ap()
    x5_o = nc.dram_tensor("x5", [TOK, D], F32, kind="ExternalOutput").ap()

    s = Sched(nc)
    with contextlib.ExitStack() as ctx:
        sb = lambda n, sh, dt: ctx.enter_context(nc.sbuf_tensor(n, sh, dt)).ap()
        c = make_consts(s, ctx, need_identity_bf=False, need_ones_row_bf=False)
        P0 = ctx.enter_context(nc.psum_tensor("P0", [128, 1024], F32)).ap()
        yz = [sb(f"yz{cc}", [128, TOK], BF16) for cc in range(DC)]
        zt = sb("zt", [128, TOK], BF16)
        szt = sb("szt", [128, TOK], BF16)
        wsc = sb("wsc", [128, D], F32)
        wbf = [sb(f"wbf{cc}", [128, D], BF16) for cc in range(DC)]
        x4t = sb("x4t", [128, D], F32)
        stg = sb("stg", [128, D], F32)

        for cc in range(DC):
            s.dma(zt, zT[cc * 128:(cc + 1) * 128, :], writes=["zt"])
            s.op("scalar", lambda e: e.activation(out=szt, in_=zt, func=AF.Silu),
                 reads=["zt"], writes=["szt"])
            s.dma(yz[cc], yT[cc * 128:(cc + 1) * 128, :], writes=[f"yz{cc}"])
            s.op("vector", lambda e, cc=cc: e.tensor_mul(yz[cc], yz[cc], szt),
                 reads=[f"yz{cc}", "szt"], writes=[f"yz{cc}"])
            s.dma(wsc, opw[cc * 128:(cc + 1) * 128, :], writes=["wsc"])
            s.op("vector", lambda e, cc=cc: e.tensor_copy(out=wbf[cc], in_=wsc),
                 reads=["wsc"], writes=[f"wbf{cc}"])
        for i in range(NT):
            for hh in range(2):
                for cc in range(DC):
                    s.op("tensor", lambda e, i=i, hh=hh, cc=cc: e.matmul(
                        P0[:, hh * 512:(hh + 1) * 512],
                        yz[cc][:, i * 128:(i + 1) * 128],
                        wbf[cc][:, hh * 512:(hh + 1) * 512],
                        start=(cc == 0), stop=(cc == DC - 1)),
                        reads=[f"yz{cc}", f"wbf{cc}"], writes=["P0"])
            s.dma(x4t, x4[i * 128:(i + 1) * 128, :], writes=["x4t"])
            s.op("vector", lambda e: e.tensor_add(stg, P0, x4t),
                 reads=["P0", "x4t"], writes=["stg"])
            s.dma(x5_o[i * 128:(i + 1) * 128, :], stg, reads=["stg"])
        s.finalize(ctx)
    return nc


def run_FGHI(x4_full, inputs):
    ncF = _get_prog("F", build_launch_F)
    ipw = np.ascontiguousarray(inputs["L2_attn_in_proj_w"].T)
    im = [{"x4": np.ascontiguousarray(x4_full[i * TOK:(i + 1) * TOK]), "ipw": ipw}
          for i in range(NCORES)]
    rF = run_bass_kernel_spmd(ncF, im, core_ids=list(range(NCORES)))
    xinT = np.concatenate([rF.results[i]["xinT"] for i in range(NCORES)], 1)
    zT = [rF.results[i]["zT"] for i in range(NCORES)]
    ncG = _get_prog("G", build_launch_G)
    cw = inputs["L2_attn_conv_w"][:, 0, :]
    cb = inputs["L2_attn_conv_b"].reshape(D, 1)
    xpwT = np.ascontiguousarray(inputs["L2_attn_x_proj_w"].T)
    im = []
    for ci in range(NCORES):
        b, hf = ci // 2, ci % 2
        rs = slice(hf * 512, (hf + 1) * 512)
        im.append({
            "xinT": np.ascontiguousarray(xinT[rs, b * 4096:(b + 1) * 4096]),
            "cw": np.ascontiguousarray(cw[rs]),
            "cb": np.ascontiguousarray(cb[rs]),
            "xpw": np.ascontiguousarray(xpwT[rs]),
        })
    rG = run_bass_kernel_spmd(ncG, im, core_ids=list(range(NCORES)))
    ncH = _get_prog("H", build_launch_H)
    alog = inputs["L2_attn_A_log"]
    dtw = inputs["L2_attn_dt_proj_w"].reshape(D, 1)
    dtb = inputs["L2_attn_dt_proj_b"].reshape(D, 1)
    im = []
    for ci in range(NCORES):
        b, hf = ci // 2, ci % 2
        rs = slice(hf * 512, (hf + 1) * 512)
        pair = ci ^ 1
        im.append({
            "xcT": rG.results[ci]["xcT"],
            "pa": rG.results[ci]["part"],
            "pb": rG.results[pair]["part"],
            "alog": np.ascontiguousarray(alog[rs]),
            "dtw": np.ascontiguousarray(dtw[rs]),
            "dtb": np.ascontiguousarray(dtb[rs]),
        })
    rH = run_bass_kernel_spmd(ncH, im, core_ids=list(range(NCORES)))
    ncI = _get_prog("I", build_launch_I)
    opw = np.ascontiguousarray(inputs["L2_attn_out_proj_w"].T)
    im = []
    for ci in range(NCORES):
        b, hf = ci // 2, ci % 2
        tslice = slice(hf * 2048, (hf + 1) * 2048)
        yT_own = np.concatenate(
            [rH.results[2 * b]["yT"][:, tslice], rH.results[2 * b + 1]["yT"][:, tslice]],
            axis=0)
        im.append({
            "yT": np.ascontiguousarray(yT_own),
            "zT": zT[ci],
            "opw": opw,
            "x4": np.ascontiguousarray(x4_full[ci * TOK:(ci + 1) * TOK]),
        })
    rI = run_bass_kernel_spmd(ncI, im, core_ids=list(range(NCORES)))
    return np.concatenate([rI.results[i]["x5"] for i in range(NCORES)], 0)



HE = 2048
HEC = HE // 128


def build_launch_J():
    nc = bass.Bass("TRN2", target_bir_lowering=False, debug=False)
    x5 = nc.dram_tensor("x5", [TOK, D], F32, kind="ExternalInput").ap()
    wg = nc.dram_tensor("wg", [D, 8], F32, kind="ExternalInput").ap()
    w1 = nc.dram_tensor("w1", [8, D, HE], F32, kind="ExternalInput").ap()
    w2 = nc.dram_tensor("w2", [8, D, HE], F32, kind="ExternalInput").ap()
    w3 = nc.dram_tensor("w3", [8, HE, D], F32, kind="ExternalInput").ap()
    x6_o = nc.dram_tensor("x6", [TOK, D], F32, kind="ExternalOutput").ap()

    s = Sched(nc)
    with contextlib.ExitStack() as ctx:
        sb = lambda n, sh, dt: ctx.enter_context(nc.sbuf_tensor(n, sh, dt)).ap()
        c = make_consts(s, ctx)
        pool = c["pool"]
        PU = ctx.enter_context(nc.psum_tensor("PU", [128, 512], F32)).ap()
        PG = ctx.enter_context(nc.psum_tensor("PG", [128, 512], F32)).ap()
        PD = ctx.enter_context(nc.psum_tensor("PD", [128, 1024], F32)).ap()
        PT = ctx.enter_context(nc.psum_tensor("PT", [128, 128], BF16)).ap()

        xin0 = sb("xinJ", [128, D], F32)
        xin = [xin0, xin0]
        xnb0 = sb("xnbJ", [128, D], BF16)
        xnb = [xnb0, xnb0]
        xn5T = [sb(f"xn5T_{cc}", [128, TOK], BF16) for cc in range(DC)]
        xn5TR = [f"xn5T_{cc}" for cc in range(DC)]
        wgf = sb("wgf", [128, 8], F32)
        wgb = sb("wgb", [128, 8 * DC], BF16)
        TS = sb("TS", [128, 48], F32)
        wts = sb("wts", [128, 8 * NT], F32)
        wsc = sb("wsc", [128, HE // 2], F32)
        w1b = [sb(f"w1b{cc}", [128, HE], BF16) for cc in range(DC)]
        w2b = [sb(f"w2b{cc}", [128, HE], BF16) for cc in range(DC)]
        w3b = [sb(f"w3b{fc}", [128, D], BF16) for fc in range(HEC)]
        gs = sb("gs", [128, 512], BF16)
        us = sb("us", [128, 512], BF16)
        gT = [sb(f"gT{fc}", [128, 512], BF16) for fc in range(HEC)]
        acc = [sb(f"acc{i}", [128, D], BF16) for i in range(NT)]
        x5t = sb("x5t", [128, D], F32)
        stg = x5t

        lns = LNStream(s, ctx, "ln5", pool, scratch=(w1b[0][:, 0:D], "w1b0"))
        for i in range(NT):
            j = i % 2
            s.dma(xin[j], x5[i * 128:(i + 1) * 128, :], writes=["xinJ"])
            lns.tile(i, xin[j], "xinJ", [(xnb[j], "xnbJ")])
            for cc in range(DC):
                s.op("tensor", lambda e, i=i, cc=cc, j=j: e.transpose(
                    PT, xnb[j][:, cc * 128:(cc + 1) * 128], c["ident_bf"]),
                    reads=["xnbJ", "c_ident_bf"], writes=["PT"])
                s.op("vector", lambda e, i=i, cc=cc: e.tensor_copy(
                    out=xn5T[cc][:, i * 128:(i + 1) * 128], in_=PT),
                    reads=["PT"], writes=[xn5TR[cc]])
        for cc in range(DC):
            s.dma(wgf, wg[cc * 128:(cc + 1) * 128, :], writes=["wgf"])
            s.op("vector", lambda e, cc=cc: e.tensor_copy(
                out=wgb[:, cc * 8:(cc + 1) * 8], in_=wgf),
                reads=["wgf"], writes=["wgb"])
        lg = TS[:, 0:8]
        m1, m2, dmm, p1, p2 = (TS[:, 8 + k:9 + k] for k in range(5))
        eq1 = TS[:, 16:24]
        eq2 = TS[:, 24:32]
        lm = TS[:, 32:40]
        t8 = TS[:, 40:48]
        for i in range(NT):
            for cc in range(DC):
                s.op("tensor", lambda e, i=i, cc=cc: e.matmul(
                    PD[:, 0:8], xn5T[cc][:, i * 128:(i + 1) * 128],
                    wgb[:, cc * 8:(cc + 1) * 8],
                    start=(cc == 0), stop=(cc == DC - 1)),
                    reads=[xn5TR[cc], "wgb"], writes=["PD"])
            s.op("vector", lambda e: e.tensor_copy(out=lg, in_=PD[:, 0:8]),
                 reads=["PD"], writes=["TS"])
            s.op("vector", lambda e: e.reduce_max(m1, lg, axis=AX.X),
                 reads=["TS"], writes=["TS"])
            s.op("vector", lambda e: e.tensor_single_scalar(
                out=eq1, in_=lg, scalar=m1, op=ALU.is_equal),
                reads=["TS"], writes=["TS"])
            s.op("vector", lambda e: e.scalar_tensor_tensor(
                out=lm, in0=eq1, scalar=-1e30, in1=lg, op0=ALU.mult, op1=ALU.add),
                reads=["TS"], writes=["TS"])
            s.op("vector", lambda e: e.reduce_max(m2, lm, axis=AX.X),
                 reads=["TS"], writes=["TS"])
            s.op("vector", lambda e: e.tensor_single_scalar(
                out=eq2, in_=lm, scalar=m2, op=ALU.is_equal),
                reads=["TS"], writes=["TS"])
            s.op("vector", lambda e: e.tensor_sub(dmm, m1, m2),
                 reads=["TS"], writes=["TS"])
            s.op("scalar", lambda e: e.activation(out=p1, in_=dmm, func=AF.Sigmoid),
                 reads=["TS"], writes=["TS"])
            s.op("vector", lambda e: e.tensor_scalar(
                out=p2, in0=p1, scalar1=-1.0, scalar2=1.0,
                op0=ALU.mult, op1=ALU.add),
                reads=["TS"], writes=["TS"])
            s.op("vector", lambda e: e.tensor_single_scalar(
                out=t8, in_=eq1, scalar=p1, op=ALU.mult),
                reads=["TS"], writes=["TS"])
            s.op("vector", lambda e, i=i: e.scalar_tensor_tensor(
                out=wts[:, i * 8:(i + 1) * 8], in0=eq2, scalar=p2, in1=t8,
                op0=ALU.mult, op1=ALU.add),
                reads=["TS"], writes=["wts"])
        for ee in range(8):
            for cc in range(DC):
                for hw in range(2):
                    hs = slice(hw * (HE // 2), (hw + 1) * (HE // 2))
                    s.dma(wsc, w1[ee, cc * 128:(cc + 1) * 128, hs], writes=["wsc"])
                    s.op("vector", lambda e, cc=cc, hs=hs: e.tensor_copy(
                        out=w1b[cc][:, hs], in_=wsc),
                        reads=["wsc"], writes=[f"w1b{cc}"])
                    s.dma(wsc, w2[ee, cc * 128:(cc + 1) * 128, hs], writes=["wsc"])
                    s.op("vector", lambda e, cc=cc, hs=hs: e.tensor_copy(
                        out=w2b[cc][:, hs], in_=wsc),
                        reads=["wsc"], writes=[f"w2b{cc}"])
            for fc in range(HEC):
                s.dma(wsc[:, 0:D], w3[ee, fc * 128:(fc + 1) * 128, :], writes=["wsc"])
                s.op("vector", lambda e, fc=fc: e.tensor_copy(
                    out=w3b[fc], in_=wsc[:, 0:D]), reads=["wsc"], writes=[f"w3b{fc}"])
            for tq in range(4):
                tsl = slice(tq * 512, (tq + 1) * 512)
                for fc in range(HEC):
                    for cc in range(DC):
                        s.op("tensor", lambda e, fc=fc, cc=cc, tsl=tsl: e.matmul(
                            PG, w1b[cc][:, fc * 128:(fc + 1) * 128],
                            xn5T[cc][:, tsl], start=(cc == 0), stop=(cc == DC - 1)),
                            reads=[f"w1b{cc}", xn5TR[cc]], writes=["PG"])
                    for cc in range(DC):
                        s.op("tensor", lambda e, fc=fc, cc=cc, tsl=tsl: e.matmul(
                            PU, w2b[cc][:, fc * 128:(fc + 1) * 128],
                            xn5T[cc][:, tsl], start=(cc == 0), stop=(cc == DC - 1)),
                            reads=[f"w2b{cc}", xn5TR[cc]], writes=["PU"])
                    s.op("scalar", lambda e: e.activation(
                        out=gs, in_=PG, func=AF.Silu), reads=["PG"], writes=["gs"])
                    s.op("vector", lambda e: e.tensor_copy(out=us, in_=PU),
                         reads=["PU"], writes=["us"])
                    s.op("vector", lambda e, fc=fc: e.tensor_mul(gT[fc], gs, us),
                         reads=["gs", "us"], writes=[f"gT{fc}"])
                for tt in range(4):
                    i = tq * 4 + tt
                    for dh in range(2):
                        for fc in range(HEC):
                            s.op("tensor", lambda e, tt=tt, dh=dh, fc=fc: e.matmul(
                                PD[:, dh * 512:(dh + 1) * 512],
                                gT[fc][:, tt * 128:(tt + 1) * 128],
                                w3b[fc][:, dh * 512:(dh + 1) * 512],
                                start=(fc == 0), stop=(fc == HEC - 1)),
                                reads=[f"gT{fc}", f"w3b{fc}"], writes=["PD"])
                    wcol = wts[:, i * 8 + ee:i * 8 + ee + 1]
                    if ee == 0:
                        s.op("vector", lambda e, i=i, wcol=wcol: e.tensor_single_scalar(
                            out=acc[i], in_=PD, scalar=wcol, op=ALU.mult),
                            reads=["PD", "wts"], writes=[f"acc{i}"])
                    else:
                        s.op("vector", lambda e, i=i, wcol=wcol: e.scalar_tensor_tensor(
                            out=acc[i], in0=PD, scalar=wcol, in1=acc[i],
                            op0=ALU.mult, op1=ALU.add),
                            reads=["PD", "wts", f"acc{i}"], writes=[f"acc{i}"])
        for i in range(NT):
            s.dma(x5t, x5[i * 128:(i + 1) * 128, :], writes=["x5t"])
            s.op("vector", lambda e, i=i: e.tensor_add(x5t, acc[i], x5t),
                 reads=[f"acc{i}", "x5t"], writes=["x5t"])
            s.dma(x6_o[i * 128:(i + 1) * 128, :], x5t, reads=["x5t"])
        s.finalize(ctx)
    return nc


def run_J(x5_full, inputs):
    nc = _get_prog("J", build_launch_J)
    im = [{"x5": np.ascontiguousarray(x5_full[i * TOK:(i + 1) * TOK]),
           "wg": np.ascontiguousarray(inputs["L2_ffn_Wg"]),
           "w1": inputs["L2_ffn_W1"], "w2": inputs["L2_ffn_W2"],
           "w3": inputs["L2_ffn_W3"]} for i in range(NCORES)]
    r = run_bass_kernel_spmd(nc, im, core_ids=list(range(NCORES)))
    return np.concatenate([r.results[i]["x6"] for i in range(NCORES)], 0)



def build_launch_K():
    nc = bass.Bass("TRN2", target_bir_lowering=False, debug=False)
    x6 = nc.dram_tensor("x6", [TOK, D], F32, kind="ExternalInput").ap()
    win = nc.dram_tensor("win", [D, D], F32, kind="ExternalInput").ap()
    bin_ = nc.dram_tensor("bin", [1, D], F32, kind="ExternalInput").ap()
    wgx = nc.dram_tensor("wgx", [D, D], F32, kind="ExternalInput").ap()
    bgx = nc.dram_tensor("bgx", [1, D], F32, kind="ExternalInput").ap()
    wga = nc.dram_tensor("wga", [D, D], F32, kind="ExternalInput").ap()
    bga = nc.dram_tensor("bga", [1, D], F32, kind="ExternalInput").ap()
    apar = nc.dram_tensor("apar", [1, D], F32, kind="ExternalInput").ap()
    a_o = nc.dram_tensor("a", [TOK, D], F32, kind="ExternalOutput").ap()
    b_o = nc.dram_tensor("b", [TOK, D], F32, kind="ExternalOutput").ap()

    s = Sched(nc)
    with contextlib.ExitStack() as ctx:
        sb = lambda n, sh, dt: ctx.enter_context(nc.sbuf_tensor(n, sh, dt)).ap()
        c = make_consts(s, ctx)
        pool = c["pool"]
        P0 = ctx.enter_context(nc.psum_tensor("P0", [128, 1024], F32)).ap()
        PT = ctx.enter_context(nc.psum_tensor("PT", [128, 128], BF16)).ap()

        xin = [sb(f"xin{j}", [128, D], F32) for j in range(2)]
        xnb = [sb(f"xnb{j}", [128, D], BF16) for j in range(2)]
        xn6T = [sb(f"xn6T_{cc}", [128, TOK], BF16) for cc in range(DC)]
        xn6TR = [f"xn6T_{cc}" for cc in range(DC)]
        p_t = [sb(f"p_{i}", [128, D], BF16) for i in range(NT)]
        pR = [f"p_{i}" for i in range(NT)]
        wsc = sb("wsc", [128, D], F32)
        wb1 = [sb(f"wb1_{cc}", [128, D], BF16) for cc in range(DC)]
        wb2 = [sb(f"wb2_{cc}", [128, D], BF16) for cc in range(DC)]
        brow = sb("brow", [1, D], F32)
        bin_bc = sb("bin_bc", [128, D], F32)
        bgx_bc = sb("bgx_bc", [128, D], F32)
        bga_bc = sb("bga_bc", [128, D], F32)
        m8bc = sb("m8bc", [128, D], F32)
        gx = sb("gx", [128, D], F32)
        ga = sb("ga", [128, D], F32)
        la = sb("la", [128, D], F32)
        av = sb("av", [128, D], F32)
        sc2 = sb("sc2", [128, D], F32)
        pre = sb("pre", [128, D], F32)
        stg = sb("stg", [128, D], F32)

        def bcast_row(hbm_row, dst, dstR, transform=None):
            s.dma(brow, hbm_row, writes=["brow"])
            if transform:
                transform()
            for hh in range(2):
                s.op("tensor", lambda e, hh=hh: e.matmul(
                    P0[:, hh * 512:(hh + 1) * 512], c["ones_row_f32"],
                    brow[:, hh * 512:(hh + 1) * 512], start=True, stop=True),
                    reads=["brow", "c_ones_row_f32"], writes=["P0"])
            s.op("vector", lambda e: e.tensor_copy(out=dst, in_=P0),
                 reads=["P0"], writes=[dstR])

        bcast_row(bin_, bin_bc, "bin_bc")
        bcast_row(bgx, bgx_bc, "bgx_bc")
        bcast_row(bga, bga_bc, "bga_bc")

        def sp_transform():
            t1r = sb("t1r", [1, D], F32)
            t2r = sb("t2r", [1, D], F32)
            one_b = pool.get(1.0)
            s.op("scalar", lambda e: e.activation(out=t1r, in_=brow, func=AF.Abs),
                 reads=["brow"], writes=["t1r"])
            s.op("scalar", lambda e: e.activation(out=t2r, in_=t1r, func=AF.Exp,
                                                  scale=-1.0),
                 reads=["t1r"], writes=["t2r"])
            s.op("scalar", lambda e, b=one_b: e.activation(
                out=t1r, in_=t2r, func=AF.Ln, bias=b[0:1, :]),
                reads=["t2r", "c_pool"], writes=["t1r"])
            s.op("scalar", lambda e: e.activation(out=t2r, in_=brow, func=AF.Relu),
                 reads=["brow"], writes=["t2r"])
            s.op("vector", lambda e: e.tensor_add(brow, t1r, t2r),
                 reads=["t1r", "t2r"], writes=["brow"])
            s.op("vector", lambda e: e.tensor_scalar_mul(brow, brow, -8.0),
                 reads=["brow"], writes=["brow"])

        bcast_row(apar, m8bc, "m8bc", transform=sp_transform)

        lns = LNStream(s, ctx, "ln6", pool)
        for i in range(NT):
            j = i % 2
            s.dma(xin[j], x6[i * 128:(i + 1) * 128, :], writes=[f"xin{j}"])
            lns.tile(i, xin[j], f"xin{j}", [(xnb[j], f"xnb{j}")])
            for cc in range(DC):
                s.op("tensor", lambda e, i=i, cc=cc, j=j: e.transpose(
                    PT, xnb[j][:, cc * 128:(cc + 1) * 128], c["ident_bf"]),
                    reads=[f"xnb{j}", "c_ident_bf"], writes=["PT"])
                s.op("vector", lambda e, i=i, cc=cc: e.tensor_copy(
                    out=xn6T[cc][:, i * 128:(i + 1) * 128], in_=PT),
                    reads=["PT"], writes=[xn6TR[cc]])
        for cc in range(DC):
            s.dma(wsc, win[cc * 128:(cc + 1) * 128, :], writes=["wsc"])
            s.op("vector", lambda e, cc=cc: e.tensor_copy(out=wb1[cc], in_=wsc),
                 reads=["wsc"], writes=[f"wb1_{cc}"])
        for i in range(NT):
            for hh in range(2):
                for cc in range(DC):
                    s.op("tensor", lambda e, i=i, hh=hh, cc=cc: e.matmul(
                        P0[:, hh * 512:(hh + 1) * 512],
                        xn6T[cc][:, i * 128:(i + 1) * 128],
                        wb1[cc][:, hh * 512:(hh + 1) * 512],
                        start=(cc == 0), stop=(cc == DC - 1)),
                        reads=[xn6TR[cc], f"wb1_{cc}"], writes=["P0"])
            s.op("vector", lambda e, i=i: e.tensor_add(p_t[i], P0, bin_bc),
                 reads=["P0", "bin_bc"], writes=[pR[i]])
        for i in range(NT):
            for cc in range(DC):
                s.op("tensor", lambda e, i=i, cc=cc: e.transpose(
                    PT, p_t[i][:, cc * 128:(cc + 1) * 128], c["ident_bf"]),
                    reads=[pR[i], "c_ident_bf"], writes=["PT"])
                s.op("vector", lambda e, i=i, cc=cc: e.tensor_copy(
                    out=xn6T[cc][:, i * 128:(i + 1) * 128], in_=PT),
                    reads=["PT"], writes=[xn6TR[cc]])
        for cc in range(DC):
            s.dma(wsc, wgx[cc * 128:(cc + 1) * 128, :], writes=["wsc"])
            s.op("vector", lambda e, cc=cc: e.tensor_copy(out=wb1[cc], in_=wsc),
                 reads=["wsc"], writes=[f"wb1_{cc}"])
            s.dma(wsc, wga[cc * 128:(cc + 1) * 128, :], writes=["wsc"])
            s.op("vector", lambda e, cc=cc: e.tensor_copy(out=wb2[cc], in_=wsc),
                 reads=["wsc"], writes=[f"wb2_{cc}"])
        for i in range(NT):
            for (wb, wbn, bbc, bbcR, dst, dstR) in (
                    (wb1, "wb1_", bgx_bc, "bgx_bc", gx, "gx"),
                    (wb2, "wb2_", bga_bc, "bga_bc", ga, "ga")):
                for hh in range(2):
                    for cc in range(DC):
                        s.op("tensor", lambda e, i=i, hh=hh, cc=cc, wb=wb: e.matmul(
                            P0[:, hh * 512:(hh + 1) * 512],
                            xn6T[cc][:, i * 128:(i + 1) * 128],
                            wb[cc][:, hh * 512:(hh + 1) * 512],
                            start=(cc == 0), stop=(cc == DC - 1)),
                            reads=[xn6TR[cc], f"{wbn}{cc}"], writes=["P0"])
                s.op("vector", lambda e, bbc=bbc: e.tensor_add(
                    pre, P0, bbc), reads=["P0", bbcR], writes=["pre"])
                s.op("scalar", lambda e, dst=dst: e.activation(
                    out=dst, in_=pre, func=AF.Sigmoid),
                    reads=["pre"], writes=[dstR])
            s.op("vector", lambda e: e.tensor_mul(la, ga, m8bc),
                 reads=["ga", "m8bc"], writes=["la"])
            s.op("scalar", lambda e: e.activation(out=av, in_=la, func=AF.Exp),
                 reads=["la"], writes=["av"])
            s.dma(a_o[i * 128:(i + 1) * 128, :], av, reads=["av"])
            s.op("scalar", lambda e: e.activation(out=sc2, in_=la, func=AF.Exp,
                                                  scale=2.0),
                 reads=["la"], writes=["sc2"])
            s.op("vector", lambda e: e.tensor_scalar(
                out=sc2, in0=sc2, scalar1=-1.0, scalar2=1.0,
                op0=ALU.mult, op1=ALU.add),
                reads=["sc2"], writes=["sc2"])
            s.op("scalar", lambda e: e.activation(out=pre, in_=sc2, func=AF.Sqrt),
                 reads=["sc2"], writes=["pre"])
            s.op("vector", lambda e, i=i: e.tensor_mul(stg, gx, p_t[i]),
                 reads=["gx", pR[i]], writes=["stg"])
            s.op("vector", lambda e: e.tensor_mul(stg, stg, pre),
                 reads=["stg", "pre"], writes=["stg"])
            s.dma(b_o[i * 128:(i + 1) * 128, :], stg, reads=["stg"])
        s.finalize(ctx)
    return nc



def build_launch_L():
    nc = bass.Bass("TRN2", target_bir_lowering=False, debug=False)
    T4 = 4096
    aT = nc.dram_tensor("aT", [512, T4], F32, kind="ExternalInput").ap()
    bT = nc.dram_tensor("bT", [512, T4], F32, kind="ExternalInput").ap()
    hT_o = nc.dram_tensor("hT", [512, T4], BF16, kind="ExternalOutput").ap()
    s = Sched(nc)
    with contextlib.ExitStack() as ctx:
        sb = lambda n, sh, dt: ctx.enter_context(nc.sbuf_tensor(n, sh, dt)).ap()
        at = sb("at", [128, T4], F32)
        bt = sb("bt", [128, T4], F32)
        ht = sb("ht", [128, T4], F32)
        hb = sb("hb", [128, T4], BF16)
        for db in range(4):
            r = db * 128
            s.dma(at, aT[r:r + 128, :], writes=["at"])
            s.dma(bt, bT[r:r + 128, :], writes=["bt"])
            s.op("vector", lambda e: e.tensor_tensor_scan(
                out=ht, data0=at, data1=bt, initial=0.0,
                op0=ALU.mult, op1=ALU.add),
                reads=["at", "bt"], writes=["ht"])
            s.op("vector", lambda e: e.tensor_copy(out=hb, in_=ht),
                 reads=["ht"], writes=["hb"])
            s.dma(hT_o[r:r + 128, :], hb, reads=["hb"])
        s.finalize(ctx)
    return nc



def build_launch_M():
    nc = bass.Bass("TRN2", target_bir_lowering=False, debug=False)
    hT = nc.dram_tensor("hT", [D, TOK], BF16, kind="ExternalInput").ap()
    wout = nc.dram_tensor("wout", [D, D], F32, kind="ExternalInput").ap()
    bout = nc.dram_tensor("bout", [1, D], F32, kind="ExternalInput").ap()
    x6 = nc.dram_tensor("x6", [TOK, D], F32, kind="ExternalInput").ap()
    x7_o = nc.dram_tensor("x7", [TOK, D], F32, kind="ExternalOutput").ap()
    s = Sched(nc)
    with contextlib.ExitStack() as ctx:
        sb = lambda n, sh, dt: ctx.enter_context(nc.sbuf_tensor(n, sh, dt)).ap()
        c = make_consts(s, ctx, need_identity_bf=False)
        P0 = ctx.enter_context(nc.psum_tensor("P0", [128, 1024], F32)).ap()
        h_sb = [sb(f"h{cc}", [128, TOK], BF16) for cc in range(DC)]
        wsc = sb("wsc", [128, D], F32)
        wbf = [sb(f"wbf{cc}", [128, D], BF16) for cc in range(DC)]
        brow = sb("brow", [1, D], F32)
        bbc = sb("bbc", [128, D], F32)
        x6t = sb("x6t", [128, D], F32)
        stg = sb("stg", [128, D], F32)
        for cc in range(DC):
            s.dma(h_sb[cc], hT[cc * 128:(cc + 1) * 128, :], writes=[f"h{cc}"])
            s.dma(wsc, wout[cc * 128:(cc + 1) * 128, :], writes=["wsc"])
            s.op("vector", lambda e, cc=cc: e.tensor_copy(out=wbf[cc], in_=wsc),
                 reads=["wsc"], writes=[f"wbf{cc}"])
        s.dma(brow, bout, writes=["brow"])
        for hh in range(2):
            s.op("tensor", lambda e, hh=hh: e.matmul(
                P0[:, hh * 512:(hh + 1) * 512], c["ones_row_f32"],
                brow[:, hh * 512:(hh + 1) * 512], start=True, stop=True),
                reads=["brow", "c_ones_row_f32"], writes=["P0"])
        s.op("vector", lambda e: e.tensor_copy(out=bbc, in_=P0),
             reads=["P0"], writes=["bbc"])
        for i in range(NT):
            for hh in range(2):
                for cc in range(DC):
                    s.op("tensor", lambda e, i=i, hh=hh, cc=cc: e.matmul(
                        P0[:, hh * 512:(hh + 1) * 512],
                        h_sb[cc][:, i * 128:(i + 1) * 128],
                        wbf[cc][:, hh * 512:(hh + 1) * 512],
                        start=(cc == 0), stop=(cc == DC - 1)),
                        reads=[f"h{cc}", f"wbf{cc}"], writes=["P0"])
            s.dma(x6t, x6[i * 128:(i + 1) * 128, :], writes=["x6t"])
            s.op("vector", lambda e: e.tensor_add(stg, P0, bbc),
                 reads=["P0", "bbc"], writes=["stg"])
            s.op("vector", lambda e: e.tensor_add(stg, stg, x6t),
                 reads=["stg", "x6t"], writes=["stg"])
            s.dma(x7_o[i * 128:(i + 1) * 128, :], stg, reads=["stg"])
        s.finalize(ctx)
    return nc



def build_launch_N():
    nc = bass.Bass("TRN2", target_bir_lowering=False, debug=False)
    x7 = nc.dram_tensor("x7", [TOK, D], F32, kind="ExternalInput").ap()
    rwt = nc.dram_tensor("rwt", [D, 8], F32, kind="ExternalInput").ap()
    rb = nc.dram_tensor("rb", [1, 8], F32, kind="ExternalInput").ap()
    we = nc.dram_tensor("we", [8, D, D], F32, kind="ExternalInput").ap()
    be = nc.dram_tensor("be", [8, D], F32, kind="ExternalInput").ap()
    x8_o = nc.dram_tensor("x8", [TOK, D], F32, kind="ExternalOutput").ap()

    s = Sched(nc)
    with contextlib.ExitStack() as ctx:
        sb = lambda n, sh, dt: ctx.enter_context(nc.sbuf_tensor(n, sh, dt)).ap()
        c = make_consts(s, ctx)
        pool = c["pool"]
        P0 = ctx.enter_context(nc.psum_tensor("P0", [128, 1024], F32)).ap()
        PT = ctx.enter_context(nc.psum_tensor("PT", [128, 128], BF16)).ap()
        xin = [sb(f"xin{j}", [128, D], F32) for j in range(2)]
        xnb = [sb(f"xnb{j}", [128, D], BF16) for j in range(2)]
        xn7T = [sb(f"xn7T_{cc}", [128, TOK], BF16) for cc in range(DC)]
        xn7TR = [f"xn7T_{cc}" for cc in range(DC)]
        rwf = sb("rwf", [128, 8], F32)
        rwb = sb("rwb", [128, 8 * DC], BF16)
        rbr = sb("rbr", [1, 8], F32)
        rb_bc = sb("rb_bc", [128, 8], F32)
        TS = sb("TS", [128, 48], F32)
        msk = sb("msk", [128, 8 * NT], F32)
        wsc = sb("wsc", [128, D], F32)
        wbf = [sb(f"wbf{cc}", [128, D], BF16) for cc in range(DC)]
        berow = sb("berow", [1, D], F32)
        bebc = sb("bebc", [128, D], F32)
        acc = [sb(f"acc{i}", [128, D], F32) for i in range(NT)]
        t1 = sb("t1", [128, D], F32)
        x7t = sb("x7t", [128, D], F32)
        stg = sb("stg", [128, D], F32)

        lns = LNStream(s, ctx, "ln7", pool)
        for i in range(NT):
            j = i % 2
            s.dma(xin[j], x7[i * 128:(i + 1) * 128, :], writes=[f"xin{j}"])
            lns.tile(i, xin[j], f"xin{j}", [(xnb[j], f"xnb{j}")])
            for cc in range(DC):
                s.op("tensor", lambda e, i=i, cc=cc, j=j: e.transpose(
                    PT, xnb[j][:, cc * 128:(cc + 1) * 128], c["ident_bf"]),
                    reads=[f"xnb{j}", "c_ident_bf"], writes=["PT"])
                s.op("vector", lambda e, i=i, cc=cc: e.tensor_copy(
                    out=xn7T[cc][:, i * 128:(i + 1) * 128], in_=PT),
                    reads=["PT"], writes=[xn7TR[cc]])
        for cc in range(DC):
            s.dma(rwf, rwt[cc * 128:(cc + 1) * 128, :], writes=["rwf"])
            s.op("vector", lambda e, cc=cc: e.tensor_copy(
                out=rwb[:, cc * 8:(cc + 1) * 8], in_=rwf),
                reads=["rwf"], writes=["rwb"])
        s.dma(rbr, rb, writes=["rbr"])
        s.op("tensor", lambda e: e.matmul(P0[:, 0:8], c["ones_row_f32"], rbr,
                                          start=True, stop=True),
             reads=["rbr", "c_ones_row_f32"], writes=["P0"])
        s.op("vector", lambda e: e.tensor_copy(out=rb_bc, in_=P0[:, 0:8]),
             reads=["P0"], writes=["rb_bc"])
        lg = TS[:, 0:8]
        m1, m2 = TS[:, 8:9], TS[:, 9:10]
        eq1 = TS[:, 16:24]
        eq2 = TS[:, 24:32]
        lm = TS[:, 32:40]
        for i in range(NT):
            for cc in range(DC):
                s.op("tensor", lambda e, i=i, cc=cc: e.matmul(
                    P0[:, 0:8], xn7T[cc][:, i * 128:(i + 1) * 128],
                    rwb[:, cc * 8:(cc + 1) * 8],
                    start=(cc == 0), stop=(cc == DC - 1)),
                    reads=[xn7TR[cc], "rwb"], writes=["P0"])
            s.op("vector", lambda e: e.tensor_add(lg, P0[:, 0:8], rb_bc),
                 reads=["P0", "rb_bc"], writes=["TS"])
            s.op("vector", lambda e: e.reduce_max(m1, lg, axis=AX.X),
                 reads=["TS"], writes=["TS"])
            s.op("vector", lambda e: e.tensor_single_scalar(
                out=eq1, in_=lg, scalar=m1, op=ALU.is_equal),
                reads=["TS"], writes=["TS"])
            s.op("vector", lambda e: e.scalar_tensor_tensor(
                out=lm, in0=eq1, scalar=-1e30, in1=lg, op0=ALU.mult, op1=ALU.add),
                reads=["TS"], writes=["TS"])
            s.op("vector", lambda e: e.reduce_max(m2, lm, axis=AX.X),
                 reads=["TS"], writes=["TS"])
            s.op("vector", lambda e: e.tensor_single_scalar(
                out=eq2, in_=lm, scalar=m2, op=ALU.is_equal),
                reads=["TS"], writes=["TS"])
            s.op("vector", lambda e, i=i: e.tensor_add(
                msk[:, i * 8:(i + 1) * 8], eq1, eq2),
                reads=["TS"], writes=["msk"])
        for ee in range(8):
            for cc in range(DC):
                s.dma(wsc, we[ee, cc * 128:(cc + 1) * 128, :], writes=["wsc"])
                s.op("vector", lambda e, cc=cc: e.tensor_copy(out=wbf[cc], in_=wsc),
                     reads=["wsc"], writes=[f"wbf{cc}"])
            s.dma(berow, be[ee:ee + 1, :], writes=["berow"])
            for hh in range(2):
                s.op("tensor", lambda e, hh=hh: e.matmul(
                    P0[:, hh * 512:(hh + 1) * 512], c["ones_row_f32"],
                    berow[:, hh * 512:(hh + 1) * 512], start=True, stop=True),
                    reads=["berow", "c_ones_row_f32"], writes=["P0"])
            s.op("vector", lambda e: e.tensor_copy(out=bebc, in_=P0),
                 reads=["P0"], writes=["bebc"])
            for i in range(NT):
                for hh in range(2):
                    for cc in range(DC):
                        s.op("tensor", lambda e, i=i, hh=hh, cc=cc: e.matmul(
                            P0[:, hh * 512:(hh + 1) * 512],
                            xn7T[cc][:, i * 128:(i + 1) * 128],
                            wbf[cc][:, hh * 512:(hh + 1) * 512],
                            start=(cc == 0), stop=(cc == DC - 1)),
                            reads=[xn7TR[cc], f"wbf{cc}"], writes=["P0"])
                s.op("vector", lambda e: e.tensor_add(t1, P0, bebc),
                     reads=["P0", "bebc"], writes=["t1"])
                wcol = msk[:, i * 8 + ee:i * 8 + ee + 1]
                if ee == 0:
                    s.op("vector", lambda e, i=i, wcol=wcol: e.tensor_single_scalar(
                        out=acc[i], in_=t1, scalar=wcol, op=ALU.mult),
                        reads=["t1", "msk"], writes=[f"acc{i}"])
                else:
                    s.op("vector", lambda e, i=i, wcol=wcol: e.scalar_tensor_tensor(
                        out=acc[i], in0=t1, scalar=wcol, in1=acc[i],
                        op0=ALU.mult, op1=ALU.add),
                        reads=["t1", "msk", f"acc{i}"], writes=[f"acc{i}"])
        for i in range(NT):
            s.dma(x7t, x7[i * 128:(i + 1) * 128, :], writes=["x7t"])
            s.op("vector", lambda e, i=i: e.tensor_add(stg, acc[i], x7t),
                 reads=[f"acc{i}", "x7t"], writes=["stg"])
            s.dma(x8_o[i * 128:(i + 1) * 128, :], stg, reads=["stg"])
        s.finalize(ctx)
    return nc


def run_KLMN(x6_full, inputs):
    ncK = _get_prog("K", build_launch_K)
    im = [{"x6": np.ascontiguousarray(x6_full[i * TOK:(i + 1) * TOK]),
           "win": inputs["L3_attn_W_in"], "bin": inputs["L3_attn_b_in"].reshape(1, D),
           "wgx": inputs["L3_attn_W_gx"], "bgx": inputs["L3_attn_b_gx"].reshape(1, D),
           "wga": inputs["L3_attn_W_ga"], "bga": inputs["L3_attn_b_ga"].reshape(1, D),
           "apar": inputs["L3_attn_a_param"].reshape(1, D)} for i in range(NCORES)]
    rK = run_bass_kernel_spmd(ncK, im, core_ids=list(range(NCORES)))
    a_full = np.concatenate([rK.results[i]["a"] for i in range(NCORES)], 0)
    b_full = np.concatenate([rK.results[i]["b"] for i in range(NCORES)], 0)
    ncL = _get_prog("L", build_launch_L)
    im = []
    for ci in range(NCORES):
        b, hf = ci // 2, ci % 2
        rs = slice(hf * 512, (hf + 1) * 512)
        aT = np.ascontiguousarray(a_full[b * 4096:(b + 1) * 4096].T[rs])
        bT = np.ascontiguousarray(b_full[b * 4096:(b + 1) * 4096].T[rs])
        im.append({"aT": aT, "bT": bT})
    rL = run_bass_kernel_spmd(ncL, im, core_ids=list(range(NCORES)))
    ncM = _get_prog("M", build_launch_M)
    im = []
    for ci in range(NCORES):
        b, hf = ci // 2, ci % 2
        tslice = slice(hf * 2048, (hf + 1) * 2048)
        hT_own = np.concatenate(
            [rL.results[2 * b]["hT"][:, tslice], rL.results[2 * b + 1]["hT"][:, tslice]],
            axis=0)
        im.append({"hT": np.ascontiguousarray(hT_own), "wout": inputs["L3_attn_W_out"],
                   "bout": inputs["L3_attn_b_out"].reshape(1, D),
                   "x6": np.ascontiguousarray(x6_full[ci * TOK:(ci + 1) * TOK])})
    rM = run_bass_kernel_spmd(ncM, im, core_ids=list(range(NCORES)))
    x7_full = np.concatenate([rM.results[i]["x7"] for i in range(NCORES)], 0)
    ncN = _get_prog("N", build_launch_N)
    rwt = np.ascontiguousarray(inputs["L3_ffn_router_weight"].T)
    im = [{"x7": np.ascontiguousarray(x7_full[i * TOK:(i + 1) * TOK]),
           "rwt": rwt, "rb": inputs["L3_ffn_router_bias"].reshape(1, 8),
           "we": inputs["L3_ffn_weight"], "be": inputs["L3_ffn_bias"]}
          for i in range(NCORES)]
    rN = run_bass_kernel_spmd(ncN, im, core_ids=list(range(NCORES)))
    return np.concatenate([rN.results[i]["x8"] for i in range(NCORES)], 0)



def _shard_tokens(a):
    flat = np.ascontiguousarray(a.reshape(-1, a.shape[-1]))
    return [flat[i * TOK:(i + 1) * TOK] for i in range(NCORES)]


_PROG_CACHE = {}


def _get_prog(name, builder):
    if name not in _PROG_CACHE:
        _PROG_CACHE[name] = builder()
    return _PROG_CACHE[name]


def run_A(x_full):
    nc = _get_prog("A", build_launch_A)
    shards = _shard_tokens(x_full)
    res = run_bass_kernel_spmd(nc, [{"x": s} for s in shards], core_ids=list(range(NCORES)))
    return np.stack([res.results[i]["part"][0] for i in range(NCORES)])


def kernel(**inputs):
    x = np.asarray(inputs["L0_attn_x"], np.float32)
    parts = run_A(x)
    b = run_B(inputs, parts)
    x1 = np.concatenate(b["x1"], 0)
    qT = np.concatenate(b["qT"], 1)
    kT = np.concatenate(b["kT"], 1)
    v = np.concatenate(b["v"], 0)
    x2 = run_C(x1, qT, kT, v)
    d1 = run_D1(x2, inputs)
    x3 = run_D2(x2, d1["qT"], d1["kT"], d1["v"], inputs)
    x4 = run_E(x3, inputs)
    x5 = run_FGHI(x4, inputs)
    x6 = run_J(x5, inputs)
    x8 = run_KLMN(x6, inputs)
    return np.ascontiguousarray(x8.reshape(4, 4096, 1024)).astype(np.float32)
